# version 03
# baseline (speedup 1.0000x reference)
import numpy as np
import ml_dtypes
import concourse.bass as bass
import concourse.mybir as mybir
from concourse.bass_utils import run_bass_kernel_spmd

F32 = mybir.dt.float32
BF16 = mybir.dt.bfloat16
F16 = mybir.dt.float16
F8 = mybir.dt.float8e4
AF = mybir.ActivationFunctionType
ALU = mybir.AluOpType
AX = mybir.AxisListType

D = 1024
S = 4096
NCORE = 8
BPC = 2
TOK = BPC * S
T = 512
NT = TOK // T
TPS = S // T
TC = T // 128
DC = D // 128
HID = 4096
HC = HID // 128
H = 16
HD = 64
LH = 8
HPAIR = 4
NL = 64
CHUNK = 64
WIN = 256

TAPS = []
N_TILES_BUILD = NT


class Buf:
    __slots__ = ("name", "w", "r")

    def __init__(self, name):
        self.name = name
        self.w = None
        self.r = []


class Prog:
    ENGS = ("pe", "act", "dve", "gp", "sp")

    def __init__(self):
        self.items = {e: [] for e in self.ENGS}
        self.ninc = {e: 0 for e in self.ENGS}
        self.idx2val = {e: {} for e in self.ENGS}
        self.flagged = {e: [] for e in self.ENGS}
        self.dma_n = {e: 0 for e in self.ENGS}
        self.water = {e: {} for e in self.ENGS}
        self.batch = {e: None for e in self.ENGS}

    def _val_for(self, e, idx):
        import bisect
        fl = self.flagged[e]
        i = bisect.bisect_left(fl, idx)
        if i < len(fl):
            return self.idx2val[e][fl[i]]
        self.ninc[e] += 1
        v = self.ninc[e]
        self.items[e][idx][2] = True
        self.idx2val[e][idx] = v
        self.flagged[e].append(idx)
        return v

    def bend(self, q):
        b = self.batch[q]
        if b is not None:
            b[0] = self.dma_n[q]
            self.batch[q] = None

    def _wait(self, ce, tok, consumer_is_dma=False):
        kind = tok[0]
        if kind == "c":
            e, idx = tok[1], tok[2]
            v = self._val_for(e, idx)
            key = "c:" + e
        else:
            _, q, b = tok
            if b[0] is None:
                self.bend(q)
            v = 16 * b[0]
            key = "d:" + q
        if self.water[ce].get(key, 0) >= v:
            return
        self.water[ce][key] = v
        self.items[ce].append(["w", key, v])

    def op(self, eng, fn, reads=(), writes=(), fence=False):
        toks = []
        for b in reads:
            if b.w is not None:
                toks.append(b.w)
        for b in writes:
            if b.w is not None:
                toks.append(b.w)
            toks.extend(b.r)
        for t in toks:
            self._wait(eng, t)
        self.items[eng].append(["i", fn, False])
        tok = ("c", eng, len(self.items[eng]) - 1, fence)
        for b in reads:
            b.r.append(tok)
        for b in writes:
            b.w = tok
            b.r = []
        return tok

    def dma(self, q, fn, reads=(), writes=()):
        toks = []
        for b in reads:
            if b.w is not None:
                toks.append(b.w)
        for b in writes:
            if b.w is not None:
                toks.append(b.w)
            toks.extend(b.r)
        for t in toks:
            self._wait(q, t, consumer_is_dma=True)
        if self.batch[q] is None:
            prev = self.dma_n[q]
            key = "d:" + q
            if prev and self.water[q].get(key, 0) < 16 * prev:
                self.water[q][key] = 16 * prev
                self.items[q].append(["w", key, 16 * prev])
            self.batch[q] = [None]
        self.items[q].append(["d", fn, False])
        self.dma_n[q] += 1
        tok = ("d", q, self.batch[q])
        for b in reads:
            b.r.append(tok)
        for b in writes:
            b.w = tok
            b.r = []
        return tok

    def replay(self, e, eng, sems, dma_sems):
        n = 0
        for it in self.items[e]:
            if it[0] == "w":
                key, v = it[1], it[2]
                if key.startswith("c:"):
                    eng.wait_ge(sems[key[2:]], v)
                else:
                    eng.wait_ge(dma_sems[key[2:]], v)
            elif it[0] == "i":
                ins = it[1](eng)
                if it[2]:
                    ins.then_inc(sems[e], 1)
                    n += 1
            else:
                ins = it[1](eng)
                ins.then_inc(dma_sems[e], 16)
        assert n == self.ninc[e], (e, n, self.ninc[e])


def _host_consts():
    c = {}
    c["ident_f32"] = np.eye(128, dtype=np.float32)
    c["ident_bf"] = np.eye(128, dtype=ml_dtypes.bfloat16)
    jj = np.arange(128)[:, None]
    ii = np.arange(128)[None, :]
    c["maskA"] = (ii >= jj).astype(ml_dtypes.bfloat16)
    c["maskC"] = (ii < jj).astype(ml_dtypes.bfloat16)
    bands = np.zeros((TPS, 64, T), dtype=ml_dtypes.bfloat16)
    for tt in range(TPS):
        n = np.arange(64)[:, None] - 8 * tt
        il = np.arange(T)[None, :] // CHUNK
        bands[tt] = (n <= il).astype(ml_dtypes.bfloat16)
    c["bands"] = bands
    e = np.zeros((16, 8 * 128), dtype=ml_dtypes.bfloat16)
    for oc in range(8):
        for p in range(128):
            e[2 * oc + p // 64, oc * 128 + p] = 1.0
    c["e_oc"] = e
    return c


def build_program(taps=(), n_tiles=NT):
    nc = bass.Bass("TRN2", target_bir_lowering=False)
    p = Prog()

    dx = nc.dram_tensor("x", [TOK, D], F32, kind="ExternalInput")
    dmask = nc.dram_tensor("dmask", [TOK, D], F16, kind="ExternalInput")
    dwq = nc.dram_tensor("wq", [D, D], F32, kind="ExternalInput")
    dwk = nc.dram_tensor("wk", [D, D], F32, kind="ExternalInput")
    dwv = nc.dram_tensor("wv", [D, D], F32, kind="ExternalInput")
    dwgq = nc.dram_tensor("wgq", [D, 512], F32, kind="ExternalInput")
    dwgk = nc.dram_tensor("wgk", [D, 512], F32, kind="ExternalInput")
    dwgv = nc.dram_tensor("wgv", [D, 512], F32, kind="ExternalInput")
    drout = nc.dram_tensor("r_out_w", [512, D], F32, kind="ExternalInput")
    dglbd = nc.dram_tensor("glbd", [D, 16], F32, kind="ExternalInput")
    dgrbd = nc.dram_tensor("grbd", [D, 16], F32, kind="ExternalInput")
    dwout = nc.dram_tensor("w_out", [D, D], F32, kind="ExternalInput")
    dw1t = nc.dram_tensor("w1t", [D, HID], F32, kind="ExternalInput")
    dw2t = nc.dram_tensor("w2t", [HID, D], F32, kind="ExternalInput")
    db1 = nc.dram_tensor("b1", [HID], F32, kind="ExternalInput")
    db2 = nc.dram_tensor("b2", [D], F32, kind="ExternalInput")
    dout = nc.dram_tensor("out", [TOK, D], F32, kind="ExternalOutput")

    s_wq = nc.dram_tensor("s_wq", [8, 128, 1024], BF16, kind="Internal")
    s_wk = nc.dram_tensor("s_wk", [8, 128, 1024], BF16, kind="Internal")
    s_wv = nc.dram_tensor("s_wv", [8, 128, 1024], BF16, kind="Internal")
    s_wout = nc.dram_tensor("s_wout", [8, 128, 1024], BF16, kind="Internal")
    s_w1 = nc.dram_tensor("s_w1", [HC, 128, 1024], F8, kind="Internal")
    s_w2 = nc.dram_tensor("s_w2", [HC, 128, 1024], F8, kind="Internal")

    CH = _host_consts()
    d_consts = {k: nc.inline_tensor(v, "c_" + k) for k, v in CH.items()}

    sb = nc.alloc_sbuf_tensor

    x_t = [sb(f"x_t{i}", [128, 1024], F32) for i in range(TC)]
    y1 = [sb(f"y1_{i}", [128, 1024], F32) for i in range(TC)]
    x_fm = [sb(f"x_fm{i}", [128, 512], BF16) for i in range(DC)]
    q_fm = [sb(f"q_fm{i}", [128, 512], BF16) for i in range(DC)]
    k_buf = [sb(f"k_buf{i}", [128, 768], BF16) for i in range(DC)]
    v_buf = [sb(f"v_buf{i}", [128, 1040], BF16) for i in range(6)]
    gq_fm = [sb(f"gq_fm{i}", [128, 512], BF16) for i in range(4)]
    e_buf = [sb(f"e_buf{i}", [128, 384], BF16) for i in range(2)]
    loc_fm = x_fm
    gated = q_fm
    rem_fm = [sb(f"rem{i}", [128, 512], BF16) for i in range(HPAIR)]
    y18 = [sb(f"y18_{i}", [128, 512], F8) for i in range(DC)]
    h8 = [sb(f"h8_{i}", [128, 512], F8) for i in range(HC)]
    h16s = [sb(f"h16s{i}", [128, 512], F16) for i in range(2)]
    y16s = [sb(f"y16s{i}", [128, 1024], F16) for i in range(2)]
    rfp = [sb(f"rfp{i}", [128, 1024], F32) for i in range(2)]
    outb = [sb(f"outb{i}", [128, 1024], F32) for i in range(1)]
    mb = [sb(f"mb{i}", [128, 1024], F16) for i in range(2)]
    gk_b = [sb(f"gk{i}", [128, 64], BF16) for i in range(HPAIR)]
    gv_f = [sb(f"gvf{i}", [128, 64], BF16) for i in range(HPAIR)]
    gv_t = [sb(f"gvt{i}", [64, 128], BF16) for i in range(HPAIR)]
    ea_b = [sb(f"ea{i}", [64, 512], BF16) for i in range(2)]
    xp_f = sb("xp_f", [128, 64], F32)
    xp_bf = sb("xp_bf", [128, 64], BF16)
    glsc = sb("glsc", [16, 512], BF16)
    grsc = sb("grsc", [16, 512], BF16)
    sr_bf = [sb(f"srbf{i}", [1, 512], BF16) for i in range(2)]
    st6 = sb("st6", [128, 12], F32)
    st2 = sb("st2", [128, 2], F32)
    rowa = sb("rowa", [128, 1], F32)
    rowb = sb("rowb", [128, 1], F32)
    scr = [sb(f"scr{i}", [128, 512], F32) for i in range(3)]
    qh = sb("qh", [128, 512], F32)
    dml = sb("dml", [128, 4], F32)
    scrb = sb("scrb", [128, 512], BF16)
    bcgr_s = sb("bcgr_s", [128, 512], BF16)
    bandv = sb("bandv", [64, 512], BF16)
    onesr128 = sb("onesr128", [128, 64], BF16)
    wsl = [sb(f"wsl{i}", [128, 1024], BF16) for i in range(2)]
    wvb = [sb(f"wvb{i}", [128, 1024], BF16) for i in range(DC)]
    wob = wvb
    w1s = [sb(f"w1s{i}", [128, 1024], F8) for i in range(2)]
    w2s = [sb(f"w2s{i}", [128, 1024], F8) for i in range(2)]
    wgq_r = [sb(f"wgq{i}", [128, 512], BF16) for i in range(DC)]
    wgk_r = [sb(f"wgk{i}", [128, 512], BF16) for i in range(DC)]
    wgv_r = [sb(f"wgv{i}", [128, 512], BF16) for i in range(DC)]
    rout_r = [sb(f"rout{i}", [128, 1024], BF16) for i in range(4)]
    glbd_r = [sb(f"glbd{i}", [128, 16], BF16) for i in range(DC)]
    grbd_r = [sb(f"grbd{i}", [128, 16], BF16) for i in range(DC)]
    b1q = sb("b1q", [128, 32], F32)
    b1q8 = sb("b1q8", [128, 32], F8)
    b2q8 = sb("b2q8", [1, 1024], F8)
    ones8 = sb("ones8", [1, 128], F8)
    onesrow = sb("onesrow", [1, 64], BF16)
    onescol = sb("onescol", [128, 1], BF16)
    ident_f32 = sb("ident_f32", [128, 128], F32)
    ident_bf = sb("ident_bf", [128, 128], BF16)
    maskA = sb("maskA", [128, 128], BF16)
    maskC = sb("maskC", [128, 128], BF16)
    e_oc = sb("e_oc", [16, 1024], BF16)
    stg = x_t[0]
    stg2 = wvb[0]
    stg8 = w1s[0]

    print("SBUF remaining bytes:", nc.sbuf_bytes_remaining)

    ps = [nc.alloc_psum_tensor(f"ps{i}", [128, 512], F32) for i in range(7)]
    psb = nc.alloc_psum_tensor("psb", [128, 1024], BF16)

    def mkbufs(ts):
        return [Buf(t.name if hasattr(t, "name") else str(i)) for i, t in enumerate(ts)]

    B = {}
    for nm, ts in [("x_t", x_t), ("y1", y1), ("x_fm", x_fm), ("q_fm", q_fm),
                   ("k_buf", k_buf), ("v_buf", v_buf), ("gq", gq_fm), ("e", e_buf),
                   ("rem", rem_fm), ("y18", y18),
                   ("h8", h8), ("h16s", h16s), ("y16s", y16s), ("rfp", rfp),
                   ("outb", outb), ("mb", mb), ("gk", gk_b), ("gvf", gv_f),
                   ("gvt", gv_t), ("ea", ea_b), ("wsl", wsl), ("wvb", wvb),
                   ("w1s", w1s), ("w2s", w2s), ("scr", scr), ("qh", [qh]),
                   ("srbf", sr_bf), ("ps", ps)]:
        B[nm] = [Buf(f"{nm}{i}") for i in range(len(ts))]
    for nm in ["xp", "glsc", "grsc", "st6", "st2",
               "rowa", "rowb", "bcgr_s", "psb", "bandv", "scrb",
               "b1q", "b2q8", "dml"]:
        B[nm] = Buf(nm)
    WR = Buf("wscratch")

    def const_load(dst, src_dram):
        p.dma("sp", lambda e, d=dst, s=src_dram: e.dma_start(out=d[:], in_=s[:]),
              writes=(Buf("c"),))

    for name, dst in [("ident_f32", ident_f32), ("ident_bf", ident_bf),
                      ("maskA", maskA), ("maskC", maskC),
                      ("e_oc", e_oc)]:
        const_load(dst, d_consts[name])
    CONST = Buf("consts")
    p.bend("sp")
    CONST.w = ("d", "sp", [p.dma_n["sp"]])

    p.op("dve", lambda e: e.memset(ones8[:], 1.0), writes=(Buf("o8"),))
    p.op("dve", lambda e: e.memset(onesrow[:], 1.0), writes=(Buf("or"),))
    p.op("dve", lambda e: e.memset(onescol[:], 1.0), writes=(Buf("oc"),))
    p.op("dve", lambda e: e.memset(onesr128[:], 1.0), writes=(Buf("o128"),))
    ONES = Buf("ones")
    ONES.w = ("c", "dve", len(p.items["dve"]) - 1)
    for i in range(6):
        vb = v_buf[i]
        p.op("dve", lambda e, vb=vb: e.memset(
            vb[:].rearrange("p (h c) -> p h c", c=65)[:, :, 64:65], 1.0),
            writes=(B["v_buf"][i],))

    C52 = 6755399441055744.0
    C23 = 12582912.0
    LOG2E_HALF = 0.72134752044448169
    LN2 = 0.69314718055994531
    p.op("dve", lambda e: e.tensor_scalar(
        out=dml[:, 0:1], in0=onescol[:], scalar1=C52, scalar2=C52,
        op0=ALU.add, op1=ALU.subtract), reads=(ONES,), writes=(B["dml"],))
    p.op("dve", lambda e: e.tensor_scalar(
        out=dml[:, 1:2], in0=dml[:, 0:1], scalar1=C52 - C23, scalar2=C23,
        op0=ALU.mult, op1=ALU.add), writes=(B["dml"],))
    p.op("dve", lambda e: e.tensor_scalar(
        out=dml[:, 2:3], in0=dml[:, 0:1], scalar1=(2.0**52) - (2.0**23),
        scalar2=2.0**23, op0=ALU.mult, op1=ALU.add), writes=(B["dml"],))
    p.op("dve", lambda e: e.tensor_scalar(
        out=dml[:, 3:4], in0=dml[:, 0:1], scalar1=C52 - C23, scalar2=C23,
        op0=ALU.mult, op1=ALU.add), writes=(B["dml"],))
    CMIX = dml[:, 1:2]
    BM2 = dml[:, 2:3]
    BQ = dml[:, 3:4]
    qcb = sb("qcb", [128, 3], F32)
    p.op("dve", lambda e: e.memset(qcb[:, 0:1], 1e-38), writes=(B["dml"],))
    p.op("dve", lambda e: e.memset(qcb[:, 1:2], -3 * LN2), writes=(B["dml"],))
    p.op("dve", lambda e: e.memset(qcb[:, 2:3], -10 * LN2), writes=(B["dml"],))

    def emit_quant(src_ap, src_buf, dst_ap, dst_buf, man, min_e, cols,
                   s0=None, s1=None, s2=None, s0b=None, s1b=None, s2b=None,
                   bias_ap=None, relu=False, parts=128):
        cs = slice(0, cols)
        if bias_ap is not None or relu:
            p.op("dve", lambda e: e.tensor_scalar(
                out=s0[0:parts, cs], in0=src_ap,
                scalar1=(bias_ap if bias_ap is not None else 0.0),
                scalar2=(0.0 if relu else None),
                op0=ALU.add, op1=(ALU.max if relu else ALU.bypass)),
                reads=(src_buf, B["b1q"]) if bias_ap is not None else (src_buf,),
                writes=(s0b,))
        else:
            p.op("dve", lambda e: e.tensor_copy(out=s0[0:parts, cs], in_=src_ap),
                 reads=(src_buf,), writes=(s0b,))
        p.op("act", lambda e: e.activation(out=s1[0:parts, cs], in_=s0[0:parts, cs],
                                           func=AF.Square),
             reads=(s0b,), writes=(s1b,))
        p.op("act", lambda e: e.activation(out=s1[0:parts, cs], in_=s1[0:parts, cs],
                                           func=AF.Ln, bias=qcb[0:parts, 0:1]),
             reads=(s1b, B["dml"]), writes=(s1b,))
        p.op("dve", lambda e: e.tensor_scalar(
            out=s2[0:parts, cs], in0=s1[0:parts, cs], scalar1=LOG2E_HALF,
            scalar2=dml[0:parts, 1:2],
            op0=ALU.mult, op1=ALU.add), reads=(s1b, B["dml"]), writes=(s2b,))
        p.op("dve", lambda e: e.tensor_scalar(
            out=s2[0:parts, cs], in0=s2[0:parts, cs], scalar1=dml[0:parts, 1:2], scalar2=None,
            op0=ALU.subtract), reads=(B["dml"],), writes=(s2b,))
        p.op("dve", lambda e: e.scalar_tensor_tensor(
            out=s1[0:parts, cs], in0=s1[0:parts, cs], scalar=LOG2E_HALF, in1=s2[0:parts, cs],
            op0=ALU.mult, op1=ALU.is_lt), reads=(s2b,), writes=(s1b,))
        p.op("dve", lambda e: e.tensor_sub(
            out=s1[0:parts, cs], in0=s2[0:parts, cs], in1=s1[0:parts, cs]),
            reads=(s2b,), writes=(s1b,))
        p.op("dve", lambda e: e.tensor_scalar(
            out=s1[0:parts, cs], in0=s1[0:parts, cs], scalar1=float(min_e), scalar2=LN2,
            op0=ALU.max, op1=ALU.mult), writes=(s1b,))
        p.op("act", lambda e: e.activation(
            out=s1[0:parts, cs], in_=s1[0:parts, cs], func=AF.Exp,
            bias=qcb[0:parts, 1:2] if man == 3 else qcb[0:parts, 2:3]),
            reads=(s1b, B["dml"]), writes=(s1b,))
        p.op("dve", lambda e: e.tensor_scalar(
            out=s2[0:parts, cs], in0=s1[0:parts, cs], scalar1=dml[0:parts, 2:3], scalar2=None,
            op0=ALU.mult), reads=(s1b, B["dml"]), writes=(s2b,))
        p.op("dve", lambda e: e.tensor_add(
            out=s1[0:parts, cs], in0=s2[0:parts, cs], in1=s1[0:parts, cs]),
            reads=(s2b,), writes=(s1b,))
        p.op("dve", lambda e: e.tensor_sub(
            out=s1[0:parts, cs], in0=s1[0:parts, cs], in1=s2[0:parts, cs]),
            reads=(s2b,), writes=(s1b,))
        p.op("dve", lambda e: e.tensor_scalar(
            out=s1[0:parts, cs], in0=s1[0:parts, cs], scalar1=dml[0:parts, 3:4], scalar2=None,
            op0=ALU.mult), reads=(B["dml"],), writes=(s1b,))
        p.op("dve", lambda e: e.tensor_add(
            out=s2[0:parts, cs], in0=s0[0:parts, cs], in1=s1[0:parts, cs]),
            reads=(s0b, s1b), writes=(s2b,))
        p.op("dve", lambda e: e.tensor_sub(
            out=dst_ap, in0=s2[0:parts, cs], in1=s1[0:parts, cs]),
            reads=(s2b, s1b), writes=(dst_buf,))

    def conv(raw_ap_fn, n_items, out_dram_fn, dt8=False, stage_cols=1024):
        for i in range(n_items):
            def _load(e, i=i):
                ap = raw_ap_fn(i)
                dst = stg[:, 0:stage_cols]
                if len(ap.shape) == 3:
                    dst = dst.rearrange("p (d c) -> p d c", c=ap.shape[2])
                return e.dma_start(out=dst, in_=ap)
            p.dma("sp", _load, writes=(B["x_t"][0],))
            dst = stg8 if dt8 else stg2
            bd = B["w1s"][0] if dt8 else B["wvb"][0]
            if dt8:
                emit_quant(stg[:, 0:stage_cols], B["x_t"][0],
                           dst[:, 0:stage_cols], bd, 3, -6, stage_cols,
                           s0=rfp[0], s1=rfp[1], s2=outb[0],
                           s0b=B["rfp"][0], s1b=B["rfp"][1], s2b=B["outb"][0])
            else:
                p.op("dve", lambda e, d=dst: e.tensor_copy(out=d[:, 0:stage_cols],
                                                           in_=stg[:, 0:stage_cols]),
                     reads=(B["x_t"][0],), writes=(bd,))
            p.dma("sp", lambda e, i=i, d=dst: e.dma_start(out=out_dram_fn(i),
                                                          in_=d[:, 0:stage_cols]),
                  reads=(bd,), writes=(WR,))

    def colslice(dr):
        def f(oc):
            return dr[:, oc * 128:(oc + 1) * 128].rearrange("(d p) c -> p d c", p=128)
        return f

    conv(colslice(dwq), 8, lambda oc: s_wq[oc])
    conv(colslice(dwk), 8, lambda oc: s_wk[oc])
    conv(lambda dc: dwv[dc * 128:(dc + 1) * 128, :], 8, lambda dc: s_wv[dc])
    conv(lambda dc: dwout[dc * 128:(dc + 1) * 128, :], 8, lambda dc: s_wout[dc])
    conv(colslice(dw1t), HC, lambda hh: s_w1[hh], dt8=True)
    conv(lambda hh: dw2t[hh * 128:(hh + 1) * 128, :], HC, lambda hh: s_w2[hh], dt8=True)

    def res_conv(raw_fn, dsts, bufs, cols):
        for i, dst in enumerate(dsts):
            p.dma("sp", lambda e, i=i: e.dma_start(out=stg[:, 0:cols], in_=raw_fn(i)),
                  writes=(B["x_t"][0],))
            p.op("dve", lambda e, d=dst: e.tensor_copy(out=d[:], in_=stg[:, 0:cols]),
                 reads=(B["x_t"][0],), writes=(bufs[i] if isinstance(bufs, list) else bufs,))

    RES = [Buf(f"res{i}") for i in range(64)]
    ri = iter(RES)
    wgq_bufs = [next(ri) for _ in range(DC)]
    wgk_bufs = [next(ri) for _ in range(DC)]
    wgv_bufs = [next(ri) for _ in range(DC)]
    rout_bufs = [next(ri) for _ in range(4)]
    glbd_bufs = [next(ri) for _ in range(DC)]
    grbd_bufs = [next(ri) for _ in range(DC)]
    res_conv(lambda i: dwgq[i * 128:(i + 1) * 128, :], wgq_r, wgq_bufs, 512)
    res_conv(lambda i: dwgk[i * 128:(i + 1) * 128, :], wgk_r, wgk_bufs, 512)
    res_conv(lambda i: dwgv[i * 128:(i + 1) * 128, :], wgv_r, wgv_bufs, 512)
    res_conv(lambda i: drout[i * 128:(i + 1) * 128, :], rout_r, rout_bufs, 1024)
    res_conv(lambda i: dglbd[i * 128:(i + 1) * 128, :], glbd_r, glbd_bufs, 16)
    res_conv(lambda i: dgrbd[i * 128:(i + 1) * 128, :], grbd_r, grbd_bufs, 16)

    B1Q8 = Buf("b1q8")
    p.dma("sp", lambda e: e.dma_start(out=stg[:, 0:32],
                                      in_=db1[:].rearrange("(h p) -> p h", p=128),
                                      allow_slow_non_contiguous=True),
          writes=(B["x_t"][0],))
    emit_quant(stg[:, 0:32], B["x_t"][0], b1q8[:], B1Q8, 3, -6, 32,
               s0=rfp[0], s1=rfp[1], s2=outb[0],
               s0b=B["rfp"][0], s1b=B["rfp"][1], s2b=B["outb"][0])
    p.op("dve", lambda e: e.tensor_copy(out=b1q[:], in_=b1q8[:]),
         reads=(B1Q8,), writes=(B["b1q"],))
    p.dma("sp", lambda e: e.dma_start(out=stg[0:1, 0:1024],
                                      in_=db2[:].rearrange("(o d) -> o d", o=1)),
          writes=(B["x_t"][0],))
    emit_quant(stg[0:1, 0:1024], B["x_t"][0], b2q8[:], B["b2q8"], 3, -6, 1024,
               s0=rfp[0], s1=rfp[1], s2=outb[0],
               s0b=B["rfp"][0], s1b=B["rfp"][1], s2b=B["outb"][0], parts=1)

    def mm(pt, pslice, lhsT, rhs, start, stop, reads, wbuf, tp=None):
        p.op("pe", lambda e: e.matmul(pslice, lhsT, rhs, start=start, stop=stop,
                                      tile_position=tp),
             reads=reads, writes=(wbuf,))

    def tap(name, sbuf_list, cols, t, bufs=None, rows=128):
        if name not in taps:
            return
        key = f"tap_{name}_{t}"
        dd = nc.dram_tensor(key, [len(sbuf_list) * rows, cols],
                            sbuf_list[0].dtype, kind="ExternalOutput")
        for i, sbt in enumerate(sbuf_list):
            rb = bufs[i] if bufs is not None else Buf("t")
            p.dma("sp", lambda e, i=i, sbt=sbt: e.dma_start(
                out=dd[i * rows:(i + 1) * rows, :],
                in_=sbt[0:rows, 0:cols]), reads=(rb,))

    def load_x(t):
        g0 = t * T
        for tc in range(TC):
            p.dma("sp", lambda e, tc=tc, g0=g0: e.dma_start(
                out=x_t[tc][:], in_=dx[g0 + tc * 128: g0 + (tc + 1) * 128, :]),
                writes=(B["x_t"][tc],))

    load_x(0)

    for t in range(n_tiles):
        seq = t // TPS
        tt = t % TPS
        Qs = tt * T
        g0 = t * T
        qc0 = Qs // 128
        nlat0 = tt * 8
        nlat = nlat0 + 8
        seq_start = (tt == 0)

        p.dma("sp", lambda e, tt=tt: e.dma_start(
            out=bandv[:], in_=d_consts["bands"][tt]), writes=(B["bandv"],))

        for dc in range(DC):
            bank = B["ps"][dc % 2]
            for tc in range(TC):
                p.op("pe", lambda e, dc=dc, tc=tc, b=ps[dc % 2]: e.transpose(
                    b[:, tc * 128:(tc + 1) * 128],
                    x_t[tc][:, dc * 128:(dc + 1) * 128], ident_f32[:]),
                    reads=(B["x_t"][tc], CONST), writes=(bank,))
            p.op("act", lambda e, dc=dc, b=ps[dc % 2]: e.copy(
                out=x_fm[dc][:], in_=b[:]),
                reads=(bank,), writes=(B["x_fm"][dc],))
        tap("x_fm", x_fm, 512, t, bufs=B["x_fm"])

        def proj_cs(scratch, outs, outbufs, kcols, wsrc_name):
            for oc in range(DC):
                wb = B["wsl"][oc % 2]
                p.dma("sp", lambda e, oc=oc, s=scratch: e.dma_start(
                    out=wsl[oc % 2][:], in_=s[oc]), reads=(WR,), writes=(wb,))
                bank = B["ps"][2 + oc % 2]
                for dc in range(DC):
                    mm(None, ps[2 + oc % 2][:, 0:512],
                       wsl[oc % 2][:, dc * 128:(dc + 1) * 128], x_fm[dc][:],
                       dc == 0, dc == DC - 1,
                       (wb, B["x_fm"][dc]), bank)
                if kcols is None:
                    p.op("act", lambda e, oc=oc, b=ps[2 + oc % 2]: e.copy(
                        out=outs[oc][:], in_=b[:, 0:512]),
                        reads=(bank,), writes=(outbufs[oc],))
                else:
                    p.op("act", lambda e, oc=oc, b=ps[2 + oc % 2]: e.copy(
                        out=outs[oc][:, 256:768], in_=b[:, 0:512]),
                        reads=(bank,), writes=(outbufs[oc],))

        proj_cs(s_wq, q_fm, B["q_fm"], None, "wq")
        proj_cs(s_wk, k_buf, B["k_buf"], 1, "wk")
        for gc in range(4):
            bank = B["ps"][4 + gc % 2]
            for dc in range(DC):
                mm(None, ps[4 + gc % 2][:, 0:512],
                   wgq_r[dc][:, gc * 128:(gc + 1) * 128], x_fm[dc][:],
                   dc == 0, dc == DC - 1,
                   (wgq_bufs[dc], B["x_fm"][dc]), bank)
            p.op("act", lambda e, gc=gc, b=ps[4 + gc % 2]: e.copy(
                out=gq_fm[gc][:], in_=b[:, 0:512]),
                reads=(bank,), writes=(B["gq"][gc],))

        for dc in range(DC):
            p.dma("sp", lambda e, dc=dc: e.dma_start(out=wvb[dc][:], in_=s_wv[dc]),
                  reads=(WR,), writes=(B["wvb"][dc],))
        for tc in range(TC):
            pk = [ps[0 + 2 * (tc % 2)], ps[1 + 2 * (tc % 2)]]
            bk = [B["ps"][0 + 2 * (tc % 2)], B["ps"][1 + 2 * (tc % 2)]]
            for dc in range(DC):
                for half in range(2):
                    mm(None, pk[half][:, 0:512],
                       x_fm[dc][:, tc * 128:(tc + 1) * 128],
                       wvb[dc][:, half * 512:(half + 1) * 512],
                       dc == 0, dc == DC - 1,
                       (B["x_fm"][dc], B["wvb"][dc]), bk[half])
            vb = 2 + tc
            for half in range(2):
                p.op("act", lambda e, tc=tc, half=half, pp=pk[half], vb=vb: e.copy(
                    out=v_buf[vb][:].rearrange("p (h c) -> p h c", c=65)
                        [:, half * 8:(half + 1) * 8, 0:64],
                    in_=pp[:].rearrange("p (h c) -> p h c", c=64)),
                    reads=(bk[half],), writes=(B["v_buf"][vb],))

        tap("q", q_fm, 512, t, bufs=B["q_fm"])
        tap("k", k_buf, 768, t, bufs=B["k_buf"])
        XPF = Buf("xpf")
        for dc in range(DC):
            p.op("dve", lambda e, dc=dc: e.tensor_reduce(
                out=xp_f[:, dc * 8:(dc + 1) * 8],
                in_=x_fm[dc][:].rearrange("p (g k) -> p g k", k=64),
                axis=AX.X, op=ALU.add),
                reads=(B["x_fm"][dc],), writes=(XPF,), fence=True)
        p.op("act", lambda e: e.activation(
            out=xp_bf[:], in_=xp_f[:], func=AF.Copy, scale=1.0 / 64),
            reads=(XPF,), writes=(B["xp"],))
        for gc in range(HPAIR):
            bank = B["ps"][2 + gc % 2]
            bankv = B["ps"][4 + gc % 2]
            for dc in range(DC):
                mm(None, ps[2 + gc % 2][:, 0:8],
                   wgk_r[dc][:, gc * 128:(gc + 1) * 128], xp_bf[:, dc * 8:(dc + 1) * 8],
                   dc == 0, dc == DC - 1, (wgk_bufs[dc], B["xp"]), bank)
                mm(None, ps[4 + gc % 2][:, 0:8],
                   wgv_r[dc][:, gc * 128:(gc + 1) * 128], xp_bf[:, dc * 8:(dc + 1) * 8],
                   dc == 0, dc == DC - 1, (wgv_bufs[dc], B["xp"]), bankv)
            p.op("act", lambda e, gc=gc, b=ps[2 + gc % 2], n0=nlat0: e.copy(
                out=gk_b[gc][:, n0:n0 + 8], in_=b[:, 0:8]),
                reads=(bank,), writes=(B["gk"][gc],))
            p.op("act", lambda e, gc=gc, b=ps[4 + gc % 2], n0=nlat0: e.copy(
                out=gv_f[gc][:, n0:n0 + 8], in_=b[:, 0:8]),
                reads=(bankv,), writes=(B["gvf"][gc],))

        tap("xp", [xp_bf], 64, t, bufs=[B["xp"]])
        tap("gk", gk_b, 64, t, bufs=B["gk"])
        tap("gvf", gv_f, 64, t, bufs=B["gvf"])
        kc_lo = max(0, qc0 - 2)
        kc_list = list(range(kc_lo, qc0 + 4))
        for h in range(H):
            ch = h // 2
            r0 = (h % 2) * 64
            avbank_i = 6 if h % 2 == 0 else 0
            avp = ps[avbank_i]
            avb = B["ps"][avbank_i]
            first_av = True
            for ik, kc in enumerate(kc_list):
                qlo = max(kc * 128, Qs)
                qhi = min(kc * 128 + 384, Qs + T)
                Wd = qhi - qlo
                cb = (kc - qc0) * 128 + 256
                sb_i = 4 + ik % 2
                scp = ps[sb_i]
                scb = B["ps"][sb_i]
                mm(None, scp[:, 0:Wd],
                   k_buf[ch][r0:r0 + 64, cb:cb + 128],
                   q_fm[ch][r0:r0 + 64, qlo - Qs:qhi - Qs],
                   True, True, (B["k_buf"][ch], B["q_fm"][ch]), scb,
                   tp=(r0, 0))
                eb = B["e"][ik % 2]
                p.op("act", lambda e, scp=scp, Wd=Wd, ik=ik: e.activation(
                    out=e_buf[ik % 2][:, 0:Wd], in_=scp[:, 0:Wd], func=AF.Exp,
                    scale=0.125), reads=(scb,), writes=(eb,))
                for s in range(Wd // 128):
                    dlt = (qlo + s * 128) // 128 - kc
                    if dlt == 1:
                        continue
                    mk = maskA if dlt == 0 else maskC
                    p.op("dve", lambda e, s=s, ik=ik, mk=mk: e.tensor_mul(
                        out=e_buf[ik % 2][:, s * 128:(s + 1) * 128],
                        in0=e_buf[ik % 2][:, s * 128:(s + 1) * 128], in1=mk[:]),
                        reads=(CONST,), writes=(eb,))
                vb = (kc - qc0) + 2
                n_subs = Wd // 128
                for s in range(n_subs):
                    qq = qlo + s * 128
                    col = qq - Qs
                    last = (ik == len(kc_list) - 1) and (s == n_subs - 1)
                    mm(None, avp[0:65, col:col + 128],
                       v_buf[vb][:, 65 * h:65 * h + 65],
                       e_buf[ik % 2][:, s * 128:(s + 1) * 128],
                       first_av, last, (B["v_buf"][vb], eb), avb)
                    first_av = False
            p.op("act", lambda e, ch=ch, r0=r0, avp=avp: e.copy(
                out=loc_fm[ch][r0:r0 + 64, :], in_=avp[0:64, 0:512]),
                reads=(avb,), writes=(B["x_fm"][ch],))
            p.op("dve", lambda e, h=h, avp=avp: e.tensor_copy(
                out=wvb[h // 2][64:65, (h % 2) * 512:(h % 2 + 1) * 512],
                in_=avp[64:65, 0:512]),
                reads=(avb,), writes=(B["wvb"][h // 2],))
        tap("v", v_buf, 1040, t, bufs=B["v_buf"])
        tap("gq", gq_fm, 512, t, bufs=B["gq"])
        tap("loc", loc_fm, 512, t, bufs=B["x_fm"])

        for hp in range(HPAIR):
            p.op("pe", lambda e, hp=hp, nl=nlat: e.transpose(
                psb[0:nl, 0:128], gv_f[hp][:, 0:nl], ident_bf[:]),
                reads=(B["gvf"][hp], CONST), writes=(B["psb"],))
            p.op("dve", lambda e, hp=hp, nl=nlat: e.tensor_copy(
                out=gv_t[hp][0:nl, :], in_=psb[0:nl, 0:128]),
                reads=(B["psb"],), writes=(B["gvt"][hp],))
            for ab in range(2):
                rr = ab * 64
                lb_i = 2 + ab
                lp = ps[lb_i]
                lb = B["ps"][lb_i]
                mm(None, lp[0:nlat, 0:512],
                   gk_b[hp][rr:rr + 64, 0:nlat], gq_fm[hp][rr:rr + 64, :],
                   True, True, (B["gk"][hp], B["gq"][hp]), lb, tp=(rr, 0))
                p.op("act", lambda e, ab=ab, lp=lp, nl=nlat: e.activation(
                    out=ea_b[ab][0:nl, :], in_=lp[0:nl, 0:512], func=AF.Exp,
                    scale=0.125), reads=(lb,), writes=(B["ea"][ab],))
                p.op("dve", lambda e, ab=ab, nl=nlat: e.tensor_mul(
                    out=ea_b[ab][0:nl, :], in0=ea_b[ab][0:nl, :],
                    in1=bandv[0:nl, :]),
                    reads=(B["bandv"],), writes=(B["ea"][ab],))
                sp_i = 4 + ab
                spp = ps[sp_i]
                spb = B["ps"][sp_i]
                mm(None, spp[0:1, 0:512], onescol[0:nlat, 0:1], ea_b[ab][0:nlat, :],
                   True, True, (ONES, B["ea"][ab]), spb)
                p.op("dve", lambda e, ab=ab, spp=spp: e.reciprocal(
                    out=scr[0][0:1, :], in_=spp[0:1, 0:512]),
                    reads=(spb,), writes=(B["scr"][0],))
                p.op("dve", lambda e, ab=ab: e.tensor_copy(
                    out=sr_bf[ab][:], in_=scr[0][0:1, :]),
                    reads=(B["scr"][0],), writes=(B["srbf"][ab],))
                bc_i = 2 + ab
                bcp = ps[bc_i]
                bcb = B["ps"][bc_i]
                mm(None, bcp[0:64, 0:512], onesrow[0:1, 0:64], sr_bf[ab][:],
                   True, True, (ONES, B["srbf"][ab]), bcb)
                p.op("dve", lambda e, ab=ab, bcp=bcp, nl=nlat: e.tensor_mul(
                    out=ea_b[ab][0:nl, :], in0=ea_b[ab][0:nl, :], in1=bcp[0:nl, 0:512]),
                    reads=(bcb,), writes=(B["ea"][ab],))
            if hp == 0:
                tap("gvt0", gv_t[0:1], 128, t, bufs=B["gvt"][0:1], rows=64)
                tap("ea0", ea_b, 512, t, bufs=B["ea"], rows=64)
            rp = ps[6]
            rb = B["ps"][6]
            mm(None, rp[0:64, 0:512], gv_t[hp][0:nlat, 0:64], ea_b[0][0:nlat, :],
               True, True, (B["gvt"][hp], B["ea"][0]), rb, tp=(0, 0))
            mm(None, rp[64:128, 0:512], gv_t[hp][0:nlat, 64:128], ea_b[1][0:nlat, :],
               True, True, (B["gvt"][hp], B["ea"][1]), rb, tp=(0, 64))
            p.op("act", lambda e, hp=hp, rp=rp: e.copy(
                out=rem_fm[hp][:], in_=rp[:, 0:512]),
                reads=(rb,), writes=(B["rem"][hp],))
        tap("rem", rem_fm, 512, t, bufs=B["rem"])

        for which, bdr, bdb, pj in [(0, glbd_r, glbd_bufs, 4), (1, grbd_r, grbd_bufs, 5)]:
            bank = B["ps"][pj]
            for dc in range(DC):
                mm(None, ps[pj][0:16, 0:512], bdr[dc][:], q_fm[dc][:],
                   dc == 0, dc == DC - 1, (bdb[dc], B["q_fm"][dc]), bank)
            if which == 0:
                p.op("act", lambda e: e.activation(
                    out=glsc[:], in_=ps[4][0:16, 0:512], func=AF.Sigmoid),
                    reads=(bank,), writes=(B["glsc"],))
            else:
                p.op("act", lambda e: e.activation(
                    out=grsc[:], in_=ps[5][0:16, 0:512], func=AF.Sigmoid),
                    reads=(bank,), writes=(B["grsc"],))

        for oc in range(DC):
            rf_i = 2 + oc % 2
            rfp_b = ps[rf_i]
            rfb = B["ps"][rf_i]
            for cc in range(4):
                mm(None, rfp_b[:, 0:512], rout_r[cc][:, oc * 128:(oc + 1) * 128],
                   rem_fm[cc][:], cc == 0, cc == 3,
                   (rout_bufs[cc], B["rem"][cc]), rfb)
            gl_i = 4 + oc % 2
            glp = ps[gl_i]
            glb = B["ps"][gl_i]
            mm(None, glp[:, 0:512], e_oc[:, oc * 128:(oc + 1) * 128], glsc[:],
               True, True, (CONST, B["glsc"]), glb)
            gr_i = 6
            grp = ps[gr_i]
            grb = B["ps"][gr_i]
            mm(None, grp[:, 0:512], e_oc[:, oc * 128:(oc + 1) * 128], grsc[:],
               True, True, (CONST, B["grsc"]), grb)
            su_i = oc % 2
            sup = ps[su_i]
            sub = B["ps"][su_i]
            mm(None, sup[0:64, 0:512], onesr128[64:65, 0:64],
               wvb[oc][64:65, 0:512], True, True,
               (ONES, B["wvb"][oc]), sub, tp=(64, 0))
            mm(None, sup[64:128, 0:512], onesr128[64:65, 0:64],
               wvb[oc][64:65, 512:1024], True, True,
               (ONES, B["wvb"][oc]), sub, tp=(64, 64))
            p.op("act", lambda e, grp=grp: e.copy(out=bcgr_s[:], in_=grp[:, 0:512]),
                 reads=(grb,), writes=(B["bcgr_s"],))
            p.op("dve", lambda e, sup=sup: e.reciprocal(
                out=scr[1][:], in_=sup[:, 0:512]),
                reads=(sub,), writes=(B["scr"][1],))
            p.op("dve", lambda e, oc=oc, glp=glp: e.tensor_mul(
                out=scr[0][:], in0=loc_fm[oc][:], in1=glp[:, 0:512]),
                reads=(B["x_fm"][oc], glb), writes=(B["scr"][0],))
            p.op("dve", lambda e: e.tensor_mul(
                out=scrb[:], in0=scr[0][:], in1=scr[1][:]),
                reads=(B["scr"][0], B["scr"][1]), writes=(B["scrb"],))
            p.op("dve", lambda e, rfp_b=rfp_b: e.tensor_mul(
                out=scr[0][:], in0=rfp_b[:, 0:512], in1=bcgr_s[:]),
                reads=(rfb, B["bcgr_s"]), writes=(B["scr"][0],))
            p.op("dve", lambda e, oc=oc: e.tensor_add(
                out=gated[oc][:], in0=scrb[:], in1=scr[0][:]),
                reads=(B["scrb"], B["scr"][0]), writes=(B["q_fm"][oc],))
        tap("gated", gated, 512, t, bufs=B["q_fm"])
        tap("glsc", [glsc], 512, t, bufs=[B["glsc"]], rows=16)
        tap("grsc", [grsc], 512, t, bufs=[B["grsc"]], rows=16)

        for dc in range(DC):
            p.dma("sp", lambda e, dc=dc: e.dma_start(out=wob[dc][:], in_=s_wout[dc]),
                  reads=(WR,), writes=(B["wvb"][dc],))

        for tc in range(TC):
            pk = [ps[0 + 2 * (tc % 2)], ps[1 + 2 * (tc % 2)]]
            bk = [B["ps"][0 + 2 * (tc % 2)], B["ps"][1 + 2 * (tc % 2)]]
            for dc in range(DC):
                for half in range(2):
                    mm(None, pk[half][:, 0:512],
                       gated[dc][:, tc * 128:(tc + 1) * 128],
                       wob[dc][:, half * 512:(half + 1) * 512],
                       dc == 0, dc == DC - 1,
                       (B["q_fm"][dc], B["wvb"][dc]), bk[half])
            ri_ = tc % 2
            for half in range(2):
                p.op("dve", lambda e, tc=tc, half=half, pp=pk[half], ri_=ri_: e.tensor_add(
                    out=rfp[ri_][:, half * 512:(half + 1) * 512],
                    in0=pp[:, 0:512],
                    in1=x_t[tc][:, half * 512:(half + 1) * 512]),
                    reads=(bk[half], B["x_t"][tc]), writes=(B["rfp"][ri_],))
            for hf in range(2):
                p.op("dve", lambda e, ri_=ri_, hf=hf: e.bn_stats(
                    out=st6[:, hf * 6:(hf + 1) * 6],
                    in_=rfp[ri_][:, hf * 512:(hf + 1) * 512]),
                    reads=(B["rfp"][ri_],), writes=(B["st6"],), fence=True)
            p.op("dve", lambda e: e.bn_aggr(out=st2[:], in_=st6[:]),
                 reads=(B["st6"],), writes=(B["st2"],), fence=True)
            p.op("dve", lambda e: e.tensor_scalar(
                out=rowa[:], in0=st2[:, 1:2], scalar1=1e-5, scalar2=None, op0=ALU.add),
                reads=(B["st2"],), writes=(B["rowa"],))
            p.op("act", lambda e: e.activation(out=rowa[:], in_=rowa[:], func=AF.Sqrt),
                 writes=(B["rowa"],))
            p.op("dve", lambda e: e.reciprocal(out=rowb[:], in_=rowa[:]),
                 reads=(B["rowa"],), writes=(B["rowb"],))
            p.op("dve", lambda e, tc=tc, ri_=ri_: e.tensor_scalar(
                out=y1[tc][:], in0=rfp[ri_][:], scalar1=st2[:, 0:1],
                scalar2=rowb[:], op0=ALU.subtract, op1=ALU.mult),
                reads=(B["rfp"][ri_], B["st2"], B["rowb"]), writes=(B["y1"][tc],))
        tap("y1", y1, 1024, t, bufs=B["y1"])

        if t + 1 < n_tiles:
            load_x(t + 1)

        for dc in range(DC):
            ti = 4 + dc % 2
            tb = B["ps"][ti]
            for tc in range(TC):
                p.op("pe", lambda e, dc=dc, tc=tc, b=ps[ti]: e.transpose(
                    b[:, tc * 128:(tc + 1) * 128],
                    y1[tc][:, dc * 128:(dc + 1) * 128], ident_f32[:]),
                    reads=(B["y1"][tc], CONST), writes=(tb,))
            emit_quant(ps[ti][:, 0:512], tb, y18[dc][:], B["y18"][dc], 3, -6,
                       512, s0=scr[0], s1=scr[1], s2=scr[2],
                       s0b=B["scr"][0], s1b=B["scr"][1], s2b=B["scr"][2])
        tap("y18", y18, 512, t, bufs=B["y18"])

        for hh in range(HC):
            wb = B["w1s"][hh % 2]
            p.dma("sp", lambda e, hh=hh: e.dma_start(out=w1s[hh % 2][:], in_=s_w1[hh]),
                  reads=(WR,), writes=(wb,))
            hi = 5 + hh % 2
            hb = B["ps"][hi]
            for dc in range(DC):
                mm(None, ps[hi][:, 0:512],
                   w1s[hh % 2][:, dc * 128:(dc + 1) * 128], y18[dc][:],
                   dc == 0, dc == DC - 1, (wb, B["y18"][dc]), hb)
            emit_quant(ps[hi][:, 0:512], hb, qh[:], B["qh"][0], 10, -14, 512,
                       s0=scr[0], s1=scr[1], s2=scr[2],
                       s0b=B["scr"][0], s1b=B["scr"][1], s2b=B["scr"][2],
                       bias_ap=b1q[:, hh:hh + 1])
            emit_quant(qh[:], B["qh"][0], h8[hh][:], B["h8"][hh], 3, -6, 512,
                       s0=scr[0], s1=scr[1], s2=scr[2],
                       s0b=B["scr"][0], s1b=B["scr"][1], s2b=B["scr"][2],
                       relu=True)

        tap("h8", h8, 512, t, bufs=B["h8"])
        if "sw1" in taps and t == 0:
            dd1 = nc.dram_tensor("tap_sw1", [2, 128, 1024], F8, kind="ExternalOutput")
            for j, hh_ in enumerate([0, 25]):
                p.dma("sp", lambda e, j=j, hh_=hh_: e.dma_start(
                    out=dd1[j], in_=s_w1[hh_]), reads=(WR,))
        for half2 in range(2):
            tcs = [2 * half2, 2 * half2 + 1]
            pk = {tc: [ps[0 + 2 * (tc % 2)], ps[1 + 2 * (tc % 2)]] for tc in tcs}
            bk = {tc: [B["ps"][0 + 2 * (tc % 2)], B["ps"][1 + 2 * (tc % 2)]] for tc in tcs}
            for tc in tcs:
                for hf in range(2):
                    mm(None, pk[tc][hf][:, 0:512], ones8[0:1, 0:128],
                       b2q8[0:1, hf * 512:(hf + 1) * 512],
                       True, False, (B["b2q8"],), bk[tc][hf])
            for hh in range(HC):
                wb = B["w2s"][hh % 2]
                p.dma("sp", lambda e, hh=hh: e.dma_start(
                    out=w2s[hh % 2][:], in_=s_w2[hh]), reads=(WR,), writes=(wb,))
                for tc in tcs:
                    for hf in range(2):
                        mm(None, pk[tc][hf][:, 0:512],
                           h8[hh][:, tc * 128:(tc + 1) * 128],
                           w2s[hh % 2][:, hf * 512:(hf + 1) * 512],
                           False, hh == HC - 1, (B["h8"][hh], wb), bk[tc][hf])
            for tc in tcs:
                mi = tc % 2
                p.dma("sp", lambda e, tc=tc, mi=mi, g0=g0: e.dma_start(
                    out=mb[mi][:], in_=dmask[g0 + tc * 128: g0 + (tc + 1) * 128, :]),
                    writes=(B["mb"][mi],))
                for hf in range(2):
                    emit_quant(pk[tc][hf][:, 0:512], bk[tc][hf],
                               y16s[mi][:, hf * 512:(hf + 1) * 512],
                               B["y16s"][mi], 10, -14, 512,
                               s0=scr[0], s1=scr[1], s2=scr[2],
                               s0b=B["scr"][0], s1b=B["scr"][1], s2b=B["scr"][2])
                ri_ = tc % 2
                p.op("dve", lambda e, mi=mi, ri_=ri_: e.tensor_mul(
                    out=rfp[ri_][:], in0=y16s[mi][:], in1=mb[mi][:]),
                    reads=(B["y16s"][mi], B["mb"][mi]), writes=(B["rfp"][ri_],))
                p.op("dve", lambda e, tc=tc, ri_=ri_: e.tensor_add(
                    out=rfp[ri_][:], in0=rfp[ri_][:], in1=y1[tc][:]),
                    reads=(B["y1"][tc],), writes=(B["rfp"][ri_],))
                for hf in range(2):
                    p.op("dve", lambda e, ri_=ri_, hf=hf: e.bn_stats(
                        out=st6[:, hf * 6:(hf + 1) * 6],
                        in_=rfp[ri_][:, hf * 512:(hf + 1) * 512]),
                        reads=(B["rfp"][ri_],), writes=(B["st6"],), fence=True)
                p.op("dve", lambda e: e.bn_aggr(out=st2[:], in_=st6[:]),
                     reads=(B["st6"],), writes=(B["st2"],), fence=True)
                p.op("dve", lambda e: e.tensor_scalar(
                    out=rowa[:], in0=st2[:, 1:2], scalar1=1e-5, scalar2=None,
                    op0=ALU.add), reads=(B["st2"],), writes=(B["rowa"],))
                p.op("act", lambda e: e.activation(out=rowa[:], in_=rowa[:],
                                                   func=AF.Sqrt), writes=(B["rowa"],))
                p.op("dve", lambda e: e.reciprocal(out=rowb[:], in_=rowa[:]),
                     reads=(B["rowa"],), writes=(B["rowb"],))
                ob = 0
                p.op("dve", lambda e, ri_=ri_, ob=ob: e.tensor_scalar(
                    out=outb[ob][:], in0=rfp[ri_][:], scalar1=st2[:, 0:1],
                    scalar2=rowb[:], op0=ALU.subtract, op1=ALU.mult),
                    reads=(B["rfp"][ri_], B["st2"], B["rowb"]),
                    writes=(B["outb"][ob],))
                p.dma("sp", lambda e, tc=tc, ob=ob, g0=g0: e.dma_start(
                    out=dout[g0 + tc * 128: g0 + (tc + 1) * 128, :], in_=outb[ob][:]),
                    reads=(B["outb"][ob],))

        if tt != TPS - 1 and t + 1 < n_tiles:
            for ch in range(DC):
                p.op("gp", lambda e, ch=ch: e.tensor_copy(
                    out=k_buf[ch][:, 0:256], in_=k_buf[ch][:, 512:768]),
                    writes=(B["k_buf"][ch],))
            for i in range(2):
                p.op("gp", lambda e, i=i: e.tensor_copy(
                    out=v_buf[i][:], in_=v_buf[i + 4][:]),
                    reads=(B["v_buf"][i + 4],), writes=(B["v_buf"][i],))

    total_sp = p.dma_n["sp"]
    p.items["sp"].append(["w", "d:sp", 16 * total_sp])
    total_gp = p.dma_n["gp"]
    if total_gp:
        p.items["gp"].append(["w", "d:gp", 16 * total_gp])
    total_aq = p.dma_n["act"]
    if total_aq:
        p.items["act"].append(["w", "d:act", 16 * total_aq])

    with (
        nc.semaphore("sem_pe") as sem_pe,
        nc.semaphore("sem_act") as sem_act,
        nc.semaphore("sem_dve") as sem_dve,
        nc.semaphore("sem_gp") as sem_gp,
        nc.semaphore("sem_sp") as sem_sp,
        nc.semaphore("dma_sp") as dma_sp,
        nc.semaphore("dma_gp") as dma_gp,
        nc.semaphore("dma_act") as dma_act,
        nc.Block() as block,
    ):
        sems = {"pe": sem_pe, "act": sem_act, "dve": sem_dve, "gp": sem_gp,
                "sp": sem_sp}
        dma_sems = {"sp": dma_sp, "gp": dma_gp, "act": dma_act}
        allnums = sorted(s.num for s in [sem_pe, sem_act, sem_dve, sem_gp,
                                         sem_sp, dma_sp, dma_gp, dma_act])
        clr = range(allnums[0], allnums[-1] + 1)

        def delay(eng, n=1200):
            for _i in range(n):
                eng.nop()

        @block.tensor
        def _(eng):
            delay(eng)
            p.replay("pe", eng, sems, dma_sems)

        @block.scalar
        def _(eng):
            delay(eng)
            p.replay("act", eng, sems, dma_sems)

        @block.vector
        def _(eng):
            delay(eng)
            p.replay("dve", eng, sems, dma_sems)

        @block.gpsimd
        def _(eng):
            eng.sem_clear(clr)
            delay(eng, 200)
            p.replay("gp", eng, sems, dma_sems)

        @block.sync
        def _(eng):
            delay(eng)
            p.replay("sp", eng, sems, dma_sems)

    return nc


_MASK_CACHE = [None]


def _dropout_mask():
    if _MASK_CACHE[0] is None:
        import jax
        with jax.default_device(jax.local_devices(backend="cpu")[0]):
            keep = np.asarray(jax.random.bernoulli(
                jax.random.key(42, impl="threefry2x32"), 0.8, (16, 4096, 1024)))
        _MASK_CACHE[0] = np.where(keep, np.float16(1.25), np.float16(0.0))
    return _MASK_CACHE[0]


def kernel(L0_attn_x, L0_attn_wq, L0_attn_wk, L0_attn_wv, L0_attn_wgq,
           L0_attn_wgk, L0_attn_wgv, L0_attn_r_out_w, L0_attn_gate_local_w,
           L0_attn_gate_remote_w, L0_attn_w_out, L0_ffn_w1, L0_ffn_b1,
           L0_ffn_w2, L0_ffn_b2):
    x = np.ascontiguousarray(L0_attn_x, dtype=np.float32)
    mask = _dropout_mask()

    glbd = np.zeros((D, 16), np.float32)
    grbd = np.zeros((D, 16), np.float32)
    for h in range(16):
        glbd[h * 64:(h + 1) * 64, h] = np.asarray(L0_attn_gate_local_w)[h]
        grbd[h * 64:(h + 1) * 64, h] = np.asarray(L0_attn_gate_remote_w)[h]

    w1t = np.ascontiguousarray(np.asarray(L0_ffn_w1, np.float32).T)
    w2t = np.ascontiguousarray(np.asarray(L0_ffn_w2, np.float32).T)

    shared = dict(
        wq=np.ascontiguousarray(L0_attn_wq, np.float32),
        wk=np.ascontiguousarray(L0_attn_wk, np.float32),
        wv=np.ascontiguousarray(L0_attn_wv, np.float32),
        wgq=np.ascontiguousarray(L0_attn_wgq, np.float32),
        wgk=np.ascontiguousarray(L0_attn_wgk, np.float32),
        wgv=np.ascontiguousarray(L0_attn_wgv, np.float32),
        r_out_w=np.ascontiguousarray(L0_attn_r_out_w, np.float32),
        glbd=glbd, grbd=grbd,
        w_out=np.ascontiguousarray(L0_attn_w_out, np.float32),
        w1t=w1t, w2t=w2t,
        b1=np.ascontiguousarray(L0_ffn_b1, np.float32),
        b2=np.ascontiguousarray(L0_ffn_b2, np.float32),
    )

    nc = build_program(taps=tuple(TAPS), n_tiles=N_TILES_BUILD)
    in_maps = []
    for c in range(NCORE):
        m = dict(shared)
        m["x"] = np.ascontiguousarray(x[c * BPC:(c + 1) * BPC].reshape(TOK, D))
        m["dmask"] = np.ascontiguousarray(mask[c * BPC:(c + 1) * BPC].reshape(TOK, D))
        in_maps.append(m)
    res = run_bass_kernel_spmd(nc, in_maps, core_ids=list(range(NCORE)))
    out = np.empty((16, 4096, 1024), np.float32)
    for c in range(NCORE):
        out[c * BPC:(c + 1) * BPC] = np.asarray(
            res.results[c]["out"], np.float32).reshape(BPC, S, D)
    _LAST_RESULTS[0] = res
    return out


_LAST_RESULTS = [None]


# version 04 (final)
# speedup vs baseline: 1.7317x; 1.7317x over previous
import numpy as np
import ml_dtypes
import concourse.bass as bass
import concourse.mybir as mybir
from concourse.bass_utils import run_bass_kernel_spmd

F32 = mybir.dt.float32
BF16 = mybir.dt.bfloat16
F16 = mybir.dt.float16
F8 = mybir.dt.float8e4
AF = mybir.ActivationFunctionType
ALU = mybir.AluOpType
AX = mybir.AxisListType

D = 1024
S = 4096
NCORE = 8
BPC = 2
TOK = BPC * S
T = 512
NT = TOK // T
TPS = S // T
TC = T // 128
DC = D // 128
HID = 4096
HC = HID // 128
H = 16
HD = 64
LH = 8
HPAIR = 4
NL = 64
CHUNK = 64
WIN = 256

TAPS = []
N_TILES_BUILD = NT


class Buf:
    __slots__ = ("name", "w", "r")

    def __init__(self, name):
        self.name = name
        self.w = None
        self.r = []


class Prog:
    ENGS = ("pe", "act", "dve", "gp", "sp")

    def __init__(self):
        self.items = {e: [] for e in self.ENGS}
        self.ninc = {e: 0 for e in self.ENGS}
        self.idx2val = {e: {} for e in self.ENGS}
        self.flagged = {e: [] for e in self.ENGS}
        self.dma_n = {e: 0 for e in self.ENGS}
        self.water = {e: {} for e in self.ENGS}
        self.batch = {e: None for e in self.ENGS}

    def _val_for(self, e, idx):
        import bisect
        fl = self.flagged[e]
        i = bisect.bisect_left(fl, idx)
        if i < len(fl):
            return self.idx2val[e][fl[i]]
        self.ninc[e] += 1
        v = self.ninc[e]
        self.items[e][idx][2] = True
        self.idx2val[e][idx] = v
        self.flagged[e].append(idx)
        return v

    def bend(self, q):
        b = self.batch[q]
        if b is not None:
            b[0] = self.dma_n[q]
            self.batch[q] = None

    def _wait(self, ce, tok, consumer_is_dma=False):
        kind = tok[0]
        if kind == "c":
            e, idx = tok[1], tok[2]
            v = self._val_for(e, idx)
            key = "c:" + e
        else:
            _, q, b = tok
            if b[0] is None:
                self.bend(q)
            v = 16 * b[0]
            key = "d:" + q
        if self.water[ce].get(key, 0) >= v:
            return
        self.water[ce][key] = v
        self.items[ce].append(["w", key, v])

    def op(self, eng, fn, reads=(), writes=(), fence=False):
        toks = []
        for b in reads:
            if b.w is not None:
                toks.append(b.w)
        for b in writes:
            if b.w is not None:
                toks.append(b.w)
            toks.extend(b.r)
        for t in toks:
            self._wait(eng, t)
        self.items[eng].append(["i", fn, False])
        tok = ("c", eng, len(self.items[eng]) - 1, fence)
        for b in reads:
            b.r.append(tok)
        for b in writes:
            b.w = tok
            b.r = []
        return tok

    def dma(self, q, fn, reads=(), writes=()):
        toks = []
        for b in reads:
            if b.w is not None:
                toks.append(b.w)
        for b in writes:
            if b.w is not None:
                toks.append(b.w)
            toks.extend(b.r)
        for t in toks:
            self._wait(q, t, consumer_is_dma=True)
        if self.batch[q] is None:
            prev = self.dma_n[q]
            key = "d:" + q
            if prev and self.water[q].get(key, 0) < 16 * prev:
                self.water[q][key] = 16 * prev
                self.items[q].append(["w", key, 16 * prev])
            self.batch[q] = [None]
        self.items[q].append(["d", fn, False])
        self.dma_n[q] += 1
        tok = ("d", q, self.batch[q])
        for b in reads:
            b.r.append(tok)
        for b in writes:
            b.w = tok
            b.r = []
        return tok

    def replay(self, e, eng, sems, dma_sems):
        n = 0
        for it in self.items[e]:
            if it[0] == "w":
                key, v = it[1], it[2]
                if key.startswith("c:"):
                    eng.wait_ge(sems[key[2:]], v)
                else:
                    eng.wait_ge(dma_sems[key[2:]], v)
            elif it[0] == "i":
                ins = it[1](eng)
                if it[2]:
                    ins.then_inc(sems[e], 1)
                    n += 1
            else:
                ins = it[1](eng)
                ins.then_inc(dma_sems[e], 16)
        assert n == self.ninc[e], (e, n, self.ninc[e])


def _host_consts():
    c = {}
    c["ident_f32"] = np.eye(128, dtype=np.float32)
    c["ident_bf"] = np.eye(128, dtype=ml_dtypes.bfloat16)
    jj = np.arange(128)[:, None]
    ii = np.arange(128)[None, :]
    c["maskA"] = (ii >= jj).astype(ml_dtypes.bfloat16)
    c["maskC"] = (ii < jj).astype(ml_dtypes.bfloat16)
    bands = np.zeros((TPS, 64, T), dtype=ml_dtypes.bfloat16)
    for tt in range(TPS):
        n = np.arange(64)[:, None] - 8 * tt
        il = np.arange(T)[None, :] // CHUNK
        bands[tt] = (n <= il).astype(ml_dtypes.bfloat16)
    c["bands"] = bands
    e = np.zeros((16, 8 * 128), dtype=ml_dtypes.bfloat16)
    for oc in range(8):
        for p in range(128):
            e[2 * oc + p // 64, oc * 128 + p] = 1.0
    c["e_oc"] = e
    return c


def build_program(taps=(), n_tiles=NT):
    nc = bass.Bass("TRN2", target_bir_lowering=False)
    p = Prog()

    dx = nc.dram_tensor("x", [TOK, D], F32, kind="ExternalInput")
    dmask = nc.dram_tensor("dmask", [TOK, D], F16, kind="ExternalInput")
    dwq = nc.dram_tensor("wq", [D, D], F32, kind="ExternalInput")
    dwk = nc.dram_tensor("wk", [D, D], F32, kind="ExternalInput")
    dwv = nc.dram_tensor("wv", [D, D], F32, kind="ExternalInput")
    dwgq = nc.dram_tensor("wgq", [D, 512], F32, kind="ExternalInput")
    dwgk = nc.dram_tensor("wgk", [D, 512], F32, kind="ExternalInput")
    dwgv = nc.dram_tensor("wgv", [D, 512], F32, kind="ExternalInput")
    drout = nc.dram_tensor("r_out_w", [512, D], F32, kind="ExternalInput")
    dglbd = nc.dram_tensor("glbd", [D, 16], F32, kind="ExternalInput")
    dgrbd = nc.dram_tensor("grbd", [D, 16], F32, kind="ExternalInput")
    dwout = nc.dram_tensor("w_out", [D, D], F32, kind="ExternalInput")
    dw1t = nc.dram_tensor("w1t", [D, HID], F32, kind="ExternalInput")
    dw2t = nc.dram_tensor("w2t", [HID, D], F32, kind="ExternalInput")
    db1 = nc.dram_tensor("b1", [HID], F32, kind="ExternalInput")
    db2 = nc.dram_tensor("b2", [D], F32, kind="ExternalInput")
    dout = nc.dram_tensor("out", [TOK, D], F32, kind="ExternalOutput")

    s_wq = nc.dram_tensor("s_wq", [8, 128, 1024], BF16, kind="Internal")
    s_wk = nc.dram_tensor("s_wk", [8, 128, 1024], BF16, kind="Internal")
    s_wv = nc.dram_tensor("s_wv", [8, 128, 1024], BF16, kind="Internal")
    s_wout = nc.dram_tensor("s_wout", [8, 128, 1024], BF16, kind="Internal")
    s_w1 = nc.dram_tensor("s_w1", [HC, 128, 1024], F8, kind="Internal")
    s_w2 = nc.dram_tensor("s_w2", [HC, 128, 1024], F8, kind="Internal")

    CH = _host_consts()
    d_consts = {k: nc.inline_tensor(v, "c_" + k) for k, v in CH.items()}

    sb = nc.alloc_sbuf_tensor

    x_t = [sb(f"x_t{i}", [128, 1024], F32) for i in range(TC)]
    y1 = [sb(f"y1_{i}", [128, 1024], F32) for i in range(TC)]
    x_fm = [sb(f"x_fm{i}", [128, 512], BF16) for i in range(DC)]
    q_fm = [sb(f"q_fm{i}", [128, 512], BF16) for i in range(DC)]
    k_buf = [sb(f"k_buf{i}", [128, 768], BF16) for i in range(DC)]
    v_buf = [sb(f"v_buf{i}", [128, 1040], BF16) for i in range(6)]
    gq_fm = [sb(f"gq_fm{i}", [128, 512], BF16) for i in range(4)]
    e_buf = [sb(f"e_buf{i}", [128, 384], BF16) for i in range(2)]
    loc_fm = x_fm
    gated = q_fm
    rem_fm = [sb(f"rem{i}", [128, 512], BF16) for i in range(HPAIR)]
    y18 = [sb(f"y18_{i}", [128, 512], F8) for i in range(DC)]
    h8 = [sb(f"h8_{i}", [128, 512], F8) for i in range(HC)]
    h16s = [sb(f"h16s{i}", [128, 512], F16) for i in range(2)]
    y16s = [sb(f"y16s{i}", [128, 1024], F16) for i in range(2)]
    rfp = [sb(f"rfp{i}", [128, 1024], F32) for i in range(2)]
    outb = [sb(f"outb{i}", [128, 1024], F32) for i in range(1)]
    mb = [sb(f"mb{i}", [128, 1024], F16) for i in range(2)]
    gk_b = [sb(f"gk{i}", [128, 64], BF16) for i in range(HPAIR)]
    gv_f = [sb(f"gvf{i}", [128, 64], BF16) for i in range(HPAIR)]
    gv_t = [sb(f"gvt{i}", [64, 128], BF16) for i in range(HPAIR)]
    ea_b = [sb(f"ea{i}", [64, 512], BF16) for i in range(2)]
    xp_f = sb("xp_f", [128, 64], F32)
    xp_bf = sb("xp_bf", [128, 64], BF16)
    glsc = sb("glsc", [16, 512], BF16)
    grsc = sb("grsc", [16, 512], BF16)
    sr_bf = [sb(f"srbf{i}", [1, 512], BF16) for i in range(2)]
    st6 = sb("st6", [128, 12], F32)
    st2 = sb("st2", [128, 2], F32)
    rowa = sb("rowa", [128, 1], F32)
    rowb = sb("rowb", [128, 1], F32)
    scr = [sb(f"scr{i}", [128, 512], F32) for i in range(2)]
    scrb = sb("scrb", [128, 512], BF16)
    bcgr_s = sb("bcgr_s", [128, 512], BF16)
    bandv = sb("bandv", [64, 512], BF16)
    onesr128 = sb("onesr128", [128, 64], BF16)
    wsl = [sb(f"wsl{i}", [128, 1024], BF16) for i in range(2)]
    wvb = [sb(f"wvb{i}", [128, 1024], BF16) for i in range(DC)]
    wob = wvb
    w1s = [sb(f"w1s{i}", [128, 1024], F8) for i in range(2)]
    w2s = [sb(f"w2s{i}", [128, 1024], F8) for i in range(2)]
    wgq_r = [sb(f"wgq{i}", [128, 512], BF16) for i in range(DC)]
    wgk_r = [sb(f"wgk{i}", [128, 512], BF16) for i in range(DC)]
    wgv_r = [sb(f"wgv{i}", [128, 512], BF16) for i in range(DC)]
    rout_r = [sb(f"rout{i}", [128, 1024], BF16) for i in range(4)]
    glbd_r = [sb(f"glbd{i}", [128, 16], BF16) for i in range(DC)]
    grbd_r = [sb(f"grbd{i}", [128, 16], BF16) for i in range(DC)]
    b1q = sb("b1q", [128, 32], F32)
    b1q8 = sb("b1q8", [128, 32], F8)
    b2q8 = sb("b2q8", [1, 1024], F8)
    ones8 = sb("ones8", [1, 128], F8)
    onesrow = sb("onesrow", [1, 64], BF16)
    onescol = sb("onescol", [128, 1], BF16)
    ident_f32 = sb("ident_f32", [128, 128], F32)
    ident_bf = sb("ident_bf", [128, 128], BF16)
    maskA = sb("maskA", [128, 128], BF16)
    maskC = sb("maskC", [128, 128], BF16)
    e_oc = sb("e_oc", [16, 1024], BF16)
    stg = x_t[0]
    stg2 = wvb[0]
    stg8 = w1s[0]

    print("SBUF remaining bytes:", nc.sbuf_bytes_remaining)

    ps = [nc.alloc_psum_tensor(f"ps{i}", [128, 512], F32) for i in range(7)]
    psb = nc.alloc_psum_tensor("psb", [128, 1024], BF16)

    def mkbufs(ts):
        return [Buf(t.name if hasattr(t, "name") else str(i)) for i, t in enumerate(ts)]

    B = {}
    for nm, ts in [("x_t", x_t), ("y1", y1), ("x_fm", x_fm), ("q_fm", q_fm),
                   ("k_buf", k_buf), ("v_buf", v_buf), ("gq", gq_fm), ("e", e_buf),
                   ("rem", rem_fm), ("y18", y18),
                   ("h8", h8), ("h16s", h16s), ("y16s", y16s), ("rfp", rfp),
                   ("outb", outb), ("mb", mb), ("gk", gk_b), ("gvf", gv_f),
                   ("gvt", gv_t), ("ea", ea_b), ("wsl", wsl), ("wvb", wvb),
                   ("w1s", w1s), ("w2s", w2s), ("scr", scr),
                   ("srbf", sr_bf), ("ps", ps)]:
        B[nm] = [Buf(f"{nm}{i}") for i in range(len(ts))]
    for nm in ["xp", "glsc", "grsc", "st6", "st2",
               "rowa", "rowb", "bcgr_s", "psb", "bandv", "scrb",
               "b1q", "b2q8"]:
        B[nm] = Buf(nm)
    WR = Buf("wscratch")

    def const_load(dst, src_dram):
        p.dma("sp", lambda e, d=dst, s=src_dram: e.dma_start(out=d[:], in_=s[:]),
              writes=(Buf("c"),))

    for name, dst in [("ident_f32", ident_f32), ("ident_bf", ident_bf),
                      ("maskA", maskA), ("maskC", maskC),
                      ("e_oc", e_oc)]:
        const_load(dst, d_consts[name])
    CONST = Buf("consts")
    p.bend("sp")
    CONST.w = ("d", "sp", [p.dma_n["sp"]])

    p.op("dve", lambda e: e.memset(ones8[:], 1.0), writes=(Buf("o8"),))
    p.op("dve", lambda e: e.memset(onesrow[:], 1.0), writes=(Buf("or"),))
    p.op("dve", lambda e: e.memset(onescol[:], 1.0), writes=(Buf("oc"),))
    p.op("dve", lambda e: e.memset(onesr128[:], 1.0), writes=(Buf("o128"),))
    ONES = Buf("ones")
    ONES.w = ("c", "dve", len(p.items["dve"]) - 1)
    for i in range(6):
        vb = v_buf[i]
        p.op("dve", lambda e, vb=vb: e.memset(
            vb[:].rearrange("p (h c) -> p h c", c=65)[:, :, 64:65], 1.0),
            writes=(B["v_buf"][i],))

    def conv(raw_ap_fn, n_items, out_dram_fn, dt8=False, stage_cols=1024):
        ins = [x_t[0], x_t[1]]
        inb = [B["x_t"][0], B["x_t"][1]]
        outs = [w1s[0], w1s[1]] if dt8 else [wvb[0], wvb[1]]
        outbs = [B["w1s"][0], B["w1s"][1]] if dt8 else [B["wvb"][0], B["wvb"][1]]

        def _load(e, i):
            ap = raw_ap_fn(i)
            dst = ins[i % 2][:, 0:stage_cols]
            if len(ap.shape) == 3:
                dst = dst.rearrange("p (d c) -> p d c", c=ap.shape[2])
            return e.dma_start(out=dst, in_=ap)

        p.dma("sp", lambda e: _load(e, 0), writes=(inb[0],))
        for i in range(n_items):
            if i + 1 < n_items:
                p.dma("sp", lambda e, i=i: _load(e, i + 1),
                      writes=(inb[(i + 1) % 2],))
            j = i % 2
            p.op("dve", lambda e, j=j: e.tensor_copy(
                out=outs[j][:, 0:stage_cols], in_=ins[j][:, 0:stage_cols]),
                reads=(inb[j],), writes=(outbs[j],))
            p.dma("sp", lambda e, i=i, j=j: e.dma_start(
                out=out_dram_fn(i), in_=outs[j][:, 0:stage_cols]),
                reads=(outbs[j],), writes=(WR,))

    def colslice(dr):
        def f(oc):
            return dr[:, oc * 128:(oc + 1) * 128].rearrange("(d p) c -> p d c", p=128)
        return f

    conv(colslice(dwq), 8, lambda oc: s_wq[oc])
    conv(colslice(dwk), 8, lambda oc: s_wk[oc])
    conv(lambda dc: dwv[dc * 128:(dc + 1) * 128, :], 8, lambda dc: s_wv[dc])
    conv(lambda dc: dwout[dc * 128:(dc + 1) * 128, :], 8, lambda dc: s_wout[dc])
    conv(colslice(dw1t), HC, lambda hh: s_w1[hh], dt8=True)
    conv(lambda hh: dw2t[hh * 128:(hh + 1) * 128, :], HC, lambda hh: s_w2[hh], dt8=True)

    def res_conv(raw_fn, dsts, bufs, cols):
        ins = [x_t[0], x_t[1]]
        inb = [B["x_t"][0], B["x_t"][1]]
        p.dma("sp", lambda e: e.dma_start(out=ins[0][:, 0:cols], in_=raw_fn(0)),
              writes=(inb[0],))
        for i, dst in enumerate(dsts):
            if i + 1 < len(dsts):
                p.dma("sp", lambda e, i=i: e.dma_start(
                    out=ins[(i + 1) % 2][:, 0:cols], in_=raw_fn(i + 1)),
                    writes=(inb[(i + 1) % 2],))
            p.op("dve", lambda e, d=dst, j=i % 2: e.tensor_copy(
                out=d[:], in_=ins[j][:, 0:cols]),
                reads=(inb[i % 2],), writes=(bufs[i] if isinstance(bufs, list) else bufs,))

    RES = [Buf(f"res{i}") for i in range(64)]
    ri = iter(RES)
    wgq_bufs = [next(ri) for _ in range(DC)]
    wgk_bufs = [next(ri) for _ in range(DC)]
    wgv_bufs = [next(ri) for _ in range(DC)]
    rout_bufs = [next(ri) for _ in range(4)]
    glbd_bufs = [next(ri) for _ in range(DC)]
    grbd_bufs = [next(ri) for _ in range(DC)]
    res_conv(lambda i: dwgq[i * 128:(i + 1) * 128, :], wgq_r, wgq_bufs, 512)
    res_conv(lambda i: dwgk[i * 128:(i + 1) * 128, :], wgk_r, wgk_bufs, 512)
    res_conv(lambda i: dwgv[i * 128:(i + 1) * 128, :], wgv_r, wgv_bufs, 512)
    res_conv(lambda i: drout[i * 128:(i + 1) * 128, :], rout_r, rout_bufs, 1024)
    res_conv(lambda i: dglbd[i * 128:(i + 1) * 128, :], glbd_r, glbd_bufs, 16)
    res_conv(lambda i: dgrbd[i * 128:(i + 1) * 128, :], grbd_r, grbd_bufs, 16)

    B1Q8 = Buf("b1q8")
    p.dma("sp", lambda e: e.dma_start(out=stg[:, 0:32],
                                      in_=db1[:].rearrange("(h p) -> p h", p=128),
                                      allow_slow_non_contiguous=True),
          writes=(B["x_t"][0],))
    p.op("dve", lambda e: e.tensor_copy(out=b1q8[:], in_=stg[:, 0:32]),
         reads=(B["x_t"][0],), writes=(B1Q8,))
    p.op("dve", lambda e: e.tensor_copy(out=b1q[:], in_=b1q8[:]),
         reads=(B1Q8,), writes=(B["b1q"],))
    p.dma("sp", lambda e: e.dma_start(out=stg[0:1, 0:1024],
                                      in_=db2[:].rearrange("(o d) -> o d", o=1)),
          writes=(B["x_t"][0],))
    p.op("dve", lambda e: e.tensor_copy(out=b2q8[:], in_=stg[0:1, 0:1024]),
         reads=(B["x_t"][0],), writes=(B["b2q8"],))

    def mm(pt, pslice, lhsT, rhs, start, stop, reads, wbuf, tp=None):
        p.op("pe", lambda e: e.matmul(pslice, lhsT, rhs, start=start, stop=stop,
                                      tile_position=tp),
             reads=reads, writes=(wbuf,))

    def tap(name, sbuf_list, cols, t, bufs=None, rows=128):
        if name not in taps:
            return
        key = f"tap_{name}_{t}"
        dd = nc.dram_tensor(key, [len(sbuf_list) * rows, cols],
                            sbuf_list[0].dtype, kind="ExternalOutput")
        for i, sbt in enumerate(sbuf_list):
            rb = bufs[i] if bufs is not None else Buf("t")
            p.dma("sp", lambda e, i=i, sbt=sbt: e.dma_start(
                out=dd[i * rows:(i + 1) * rows, :],
                in_=sbt[0:rows, 0:cols]), reads=(rb,))

    def load_x(t):
        g0 = t * T
        for tc in range(TC):
            p.dma("sp", lambda e, tc=tc, g0=g0: e.dma_start(
                out=x_t[tc][:], in_=dx[g0 + tc * 128: g0 + (tc + 1) * 128, :]),
                writes=(B["x_t"][tc],))

    load_x(0)

    for t in range(n_tiles):
        seq = t // TPS
        tt = t % TPS
        Qs = tt * T
        g0 = t * T
        qc0 = Qs // 128
        nlat0 = tt * 8
        nlat = nlat0 + 8
        seq_start = (tt == 0)

        p.dma("sp", lambda e, tt=tt: e.dma_start(
            out=bandv[:], in_=d_consts["bands"][tt]), writes=(B["bandv"],))

        for dc in range(DC):
            bank = B["ps"][dc % 2]
            for tc in range(TC):
                p.op("pe", lambda e, dc=dc, tc=tc, b=ps[dc % 2]: e.transpose(
                    b[:, tc * 128:(tc + 1) * 128],
                    x_t[tc][:, dc * 128:(dc + 1) * 128], ident_f32[:]),
                    reads=(B["x_t"][tc], CONST), writes=(bank,))
            p.op("act", lambda e, dc=dc, b=ps[dc % 2]: e.copy(
                out=x_fm[dc][:], in_=b[:]),
                reads=(bank,), writes=(B["x_fm"][dc],))
        tap("x_fm", x_fm, 512, t, bufs=B["x_fm"])

        def proj_cs(scratch, outs, outbufs, kcols, wsrc_name):
            for oc in range(DC):
                wb = B["wsl"][oc % 2]
                p.dma("sp", lambda e, oc=oc, s=scratch: e.dma_start(
                    out=wsl[oc % 2][:], in_=s[oc]), reads=(WR,), writes=(wb,))
                bank = B["ps"][2 + oc % 2]
                for dc in range(DC):
                    mm(None, ps[2 + oc % 2][:, 0:512],
                       wsl[oc % 2][:, dc * 128:(dc + 1) * 128], x_fm[dc][:],
                       dc == 0, dc == DC - 1,
                       (wb, B["x_fm"][dc]), bank)
                if kcols is None:
                    p.op("act", lambda e, oc=oc, b=ps[2 + oc % 2]: e.copy(
                        out=outs[oc][:], in_=b[:, 0:512]),
                        reads=(bank,), writes=(outbufs[oc],))
                else:
                    p.op("act", lambda e, oc=oc, b=ps[2 + oc % 2]: e.copy(
                        out=outs[oc][:, 256:768], in_=b[:, 0:512]),
                        reads=(bank,), writes=(outbufs[oc],))

        proj_cs(s_wq, q_fm, B["q_fm"], None, "wq")
        proj_cs(s_wk, k_buf, B["k_buf"], 1, "wk")
        for gc in range(4):
            bank = B["ps"][4 + gc % 2]
            for dc in range(DC):
                mm(None, ps[4 + gc % 2][:, 0:512],
                   wgq_r[dc][:, gc * 128:(gc + 1) * 128], x_fm[dc][:],
                   dc == 0, dc == DC - 1,
                   (wgq_bufs[dc], B["x_fm"][dc]), bank)
            p.op("act", lambda e, gc=gc, b=ps[4 + gc % 2]: e.copy(
                out=gq_fm[gc][:], in_=b[:, 0:512]),
                reads=(bank,), writes=(B["gq"][gc],))

        for dc in range(DC):
            p.dma("sp", lambda e, dc=dc: e.dma_start(out=wvb[dc][:], in_=s_wv[dc]),
                  reads=(WR,), writes=(B["wvb"][dc],))
        for tc in range(TC):
            pk = [ps[0 + 2 * (tc % 2)], ps[1 + 2 * (tc % 2)]]
            bk = [B["ps"][0 + 2 * (tc % 2)], B["ps"][1 + 2 * (tc % 2)]]
            for dc in range(DC):
                for half in range(2):
                    mm(None, pk[half][:, 0:512],
                       x_fm[dc][:, tc * 128:(tc + 1) * 128],
                       wvb[dc][:, half * 512:(half + 1) * 512],
                       dc == 0, dc == DC - 1,
                       (B["x_fm"][dc], B["wvb"][dc]), bk[half])
            vb = 2 + tc
            for half in range(2):
                p.op("act", lambda e, tc=tc, half=half, pp=pk[half], vb=vb: e.copy(
                    out=v_buf[vb][:].rearrange("p (h c) -> p h c", c=65)
                        [:, half * 8:(half + 1) * 8, 0:64],
                    in_=pp[:].rearrange("p (h c) -> p h c", c=64)),
                    reads=(bk[half],), writes=(B["v_buf"][vb],))

        tap("q", q_fm, 512, t, bufs=B["q_fm"])
        tap("k", k_buf, 768, t, bufs=B["k_buf"])
        XPF = Buf("xpf")
        for dc in range(DC):
            p.op("dve", lambda e, dc=dc: e.tensor_reduce(
                out=xp_f[:, dc * 8:(dc + 1) * 8],
                in_=x_fm[dc][:].rearrange("p (g k) -> p g k", k=64),
                axis=AX.X, op=ALU.add),
                reads=(B["x_fm"][dc],), writes=(XPF,), fence=True)
        p.op("act", lambda e: e.activation(
            out=xp_bf[:], in_=xp_f[:], func=AF.Copy, scale=1.0 / 64),
            reads=(XPF,), writes=(B["xp"],))
        for gc in range(HPAIR):
            bank = B["ps"][2 + gc % 2]
            bankv = B["ps"][4 + gc % 2]
            for dc in range(DC):
                mm(None, ps[2 + gc % 2][:, 0:8],
                   wgk_r[dc][:, gc * 128:(gc + 1) * 128], xp_bf[:, dc * 8:(dc + 1) * 8],
                   dc == 0, dc == DC - 1, (wgk_bufs[dc], B["xp"]), bank)
                mm(None, ps[4 + gc % 2][:, 0:8],
                   wgv_r[dc][:, gc * 128:(gc + 1) * 128], xp_bf[:, dc * 8:(dc + 1) * 8],
                   dc == 0, dc == DC - 1, (wgv_bufs[dc], B["xp"]), bankv)
            p.op("act", lambda e, gc=gc, b=ps[2 + gc % 2], n0=nlat0: e.copy(
                out=gk_b[gc][:, n0:n0 + 8], in_=b[:, 0:8]),
                reads=(bank,), writes=(B["gk"][gc],))
            p.op("act", lambda e, gc=gc, b=ps[4 + gc % 2], n0=nlat0: e.copy(
                out=gv_f[gc][:, n0:n0 + 8], in_=b[:, 0:8]),
                reads=(bankv,), writes=(B["gvf"][gc],))

        tap("xp", [xp_bf], 64, t, bufs=[B["xp"]])
        tap("gk", gk_b, 64, t, bufs=B["gk"])
        tap("gvf", gv_f, 64, t, bufs=B["gvf"])
        kc_lo = max(0, qc0 - 2)
        kc_list = list(range(kc_lo, qc0 + 4))
        for h in range(H):
            ch = h // 2
            r0 = (h % 2) * 64
            avbank_i = 6 if h % 2 == 0 else 0
            avp = ps[avbank_i]
            avb = B["ps"][avbank_i]
            first_av = True
            for ik, kc in enumerate(kc_list):
                qlo = max(kc * 128, Qs)
                qhi = min(kc * 128 + 384, Qs + T)
                Wd = qhi - qlo
                cb = (kc - qc0) * 128 + 256
                sb_i = 4 + ik % 2
                scp = ps[sb_i]
                scb = B["ps"][sb_i]
                mm(None, scp[:, 0:Wd],
                   k_buf[ch][r0:r0 + 64, cb:cb + 128],
                   q_fm[ch][r0:r0 + 64, qlo - Qs:qhi - Qs],
                   True, True, (B["k_buf"][ch], B["q_fm"][ch]), scb,
                   tp=(r0, 0))
                eb = B["e"][ik % 2]
                p.op("act", lambda e, scp=scp, Wd=Wd, ik=ik: e.activation(
                    out=e_buf[ik % 2][:, 0:Wd], in_=scp[:, 0:Wd], func=AF.Exp,
                    scale=0.125), reads=(scb,), writes=(eb,))
                for s in range(Wd // 128):
                    dlt = (qlo + s * 128) // 128 - kc
                    if dlt == 1:
                        continue
                    mk = maskA if dlt == 0 else maskC
                    p.op("dve", lambda e, s=s, ik=ik, mk=mk: e.tensor_mul(
                        out=e_buf[ik % 2][:, s * 128:(s + 1) * 128],
                        in0=e_buf[ik % 2][:, s * 128:(s + 1) * 128], in1=mk[:]),
                        reads=(CONST,), writes=(eb,))
                vb = (kc - qc0) + 2
                n_subs = Wd // 128
                for s in range(n_subs):
                    qq = qlo + s * 128
                    col = qq - Qs
                    last = (ik == len(kc_list) - 1) and (s == n_subs - 1)
                    mm(None, avp[0:65, col:col + 128],
                       v_buf[vb][:, 65 * h:65 * h + 65],
                       e_buf[ik % 2][:, s * 128:(s + 1) * 128],
                       first_av, last, (B["v_buf"][vb], eb), avb)
                    first_av = False
            p.op("act", lambda e, ch=ch, r0=r0, avp=avp: e.copy(
                out=loc_fm[ch][r0:r0 + 64, :], in_=avp[0:64, 0:512]),
                reads=(avb,), writes=(B["x_fm"][ch],))
            p.op("dve", lambda e, h=h, avp=avp: e.tensor_copy(
                out=wvb[h // 2][64:65, (h % 2) * 512:(h % 2 + 1) * 512],
                in_=avp[64:65, 0:512]),
                reads=(avb,), writes=(B["wvb"][h // 2],))
        tap("v", v_buf, 1040, t, bufs=B["v_buf"])
        tap("gq", gq_fm, 512, t, bufs=B["gq"])
        tap("loc", loc_fm, 512, t, bufs=B["x_fm"])

        for hp in range(HPAIR):
            p.op("pe", lambda e, hp=hp, nl=nlat: e.transpose(
                psb[0:nl, 0:128], gv_f[hp][:, 0:nl], ident_bf[:]),
                reads=(B["gvf"][hp], CONST), writes=(B["psb"],))
            p.op("dve", lambda e, hp=hp, nl=nlat: e.tensor_copy(
                out=gv_t[hp][0:nl, :], in_=psb[0:nl, 0:128]),
                reads=(B["psb"],), writes=(B["gvt"][hp],))
            for ab in range(2):
                rr = ab * 64
                lb_i = 2 + ab
                lp = ps[lb_i]
                lb = B["ps"][lb_i]
                mm(None, lp[0:nlat, 0:512],
                   gk_b[hp][rr:rr + 64, 0:nlat], gq_fm[hp][rr:rr + 64, :],
                   True, True, (B["gk"][hp], B["gq"][hp]), lb, tp=(rr, 0))
                p.op("act", lambda e, ab=ab, lp=lp, nl=nlat: e.activation(
                    out=ea_b[ab][0:nl, :], in_=lp[0:nl, 0:512], func=AF.Exp,
                    scale=0.125), reads=(lb,), writes=(B["ea"][ab],))
                p.op("dve", lambda e, ab=ab, nl=nlat: e.tensor_mul(
                    out=ea_b[ab][0:nl, :], in0=ea_b[ab][0:nl, :],
                    in1=bandv[0:nl, :]),
                    reads=(B["bandv"],), writes=(B["ea"][ab],))
                sp_i = 4 + ab
                spp = ps[sp_i]
                spb = B["ps"][sp_i]
                mm(None, spp[0:1, 0:512], onescol[0:nlat, 0:1], ea_b[ab][0:nlat, :],
                   True, True, (ONES, B["ea"][ab]), spb)
                p.op("dve", lambda e, ab=ab, spp=spp: e.reciprocal(
                    out=scr[0][0:1, :], in_=spp[0:1, 0:512]),
                    reads=(spb,), writes=(B["scr"][0],))
                p.op("dve", lambda e, ab=ab: e.tensor_copy(
                    out=sr_bf[ab][:], in_=scr[0][0:1, :]),
                    reads=(B["scr"][0],), writes=(B["srbf"][ab],))
                bc_i = 2 + ab
                bcp = ps[bc_i]
                bcb = B["ps"][bc_i]
                mm(None, bcp[0:64, 0:512], onesrow[0:1, 0:64], sr_bf[ab][:],
                   True, True, (ONES, B["srbf"][ab]), bcb)
                p.op("dve", lambda e, ab=ab, bcp=bcp, nl=nlat: e.tensor_mul(
                    out=ea_b[ab][0:nl, :], in0=ea_b[ab][0:nl, :], in1=bcp[0:nl, 0:512]),
                    reads=(bcb,), writes=(B["ea"][ab],))
            if hp == 0:
                tap("gvt0", gv_t[0:1], 128, t, bufs=B["gvt"][0:1], rows=64)
                tap("ea0", ea_b, 512, t, bufs=B["ea"], rows=64)
            rp = ps[6]
            rb = B["ps"][6]
            mm(None, rp[0:64, 0:512], gv_t[hp][0:nlat, 0:64], ea_b[0][0:nlat, :],
               True, True, (B["gvt"][hp], B["ea"][0]), rb, tp=(0, 0))
            mm(None, rp[64:128, 0:512], gv_t[hp][0:nlat, 64:128], ea_b[1][0:nlat, :],
               True, True, (B["gvt"][hp], B["ea"][1]), rb, tp=(0, 64))
            p.op("act", lambda e, hp=hp, rp=rp: e.copy(
                out=rem_fm[hp][:], in_=rp[:, 0:512]),
                reads=(rb,), writes=(B["rem"][hp],))
        tap("rem", rem_fm, 512, t, bufs=B["rem"])

        for which, bdr, bdb, pj in [(0, glbd_r, glbd_bufs, 4), (1, grbd_r, grbd_bufs, 5)]:
            bank = B["ps"][pj]
            for dc in range(DC):
                mm(None, ps[pj][0:16, 0:512], bdr[dc][:], q_fm[dc][:],
                   dc == 0, dc == DC - 1, (bdb[dc], B["q_fm"][dc]), bank)
            if which == 0:
                p.op("act", lambda e: e.activation(
                    out=glsc[:], in_=ps[4][0:16, 0:512], func=AF.Sigmoid),
                    reads=(bank,), writes=(B["glsc"],))
            else:
                p.op("act", lambda e: e.activation(
                    out=grsc[:], in_=ps[5][0:16, 0:512], func=AF.Sigmoid),
                    reads=(bank,), writes=(B["grsc"],))

        for oc in range(DC):
            rf_i = 2 + oc % 2
            rfp_b = ps[rf_i]
            rfb = B["ps"][rf_i]
            for cc in range(4):
                mm(None, rfp_b[:, 0:512], rout_r[cc][:, oc * 128:(oc + 1) * 128],
                   rem_fm[cc][:], cc == 0, cc == 3,
                   (rout_bufs[cc], B["rem"][cc]), rfb)
            gl_i = 4 + oc % 2
            glp = ps[gl_i]
            glb = B["ps"][gl_i]
            mm(None, glp[:, 0:512], e_oc[:, oc * 128:(oc + 1) * 128], glsc[:],
               True, True, (CONST, B["glsc"]), glb)
            gr_i = 6
            grp = ps[gr_i]
            grb = B["ps"][gr_i]
            mm(None, grp[:, 0:512], e_oc[:, oc * 128:(oc + 1) * 128], grsc[:],
               True, True, (CONST, B["grsc"]), grb)
            su_i = oc % 2
            sup = ps[su_i]
            sub = B["ps"][su_i]
            mm(None, sup[0:64, 0:512], onesr128[64:65, 0:64],
               wvb[oc][64:65, 0:512], True, True,
               (ONES, B["wvb"][oc]), sub, tp=(64, 0))
            mm(None, sup[64:128, 0:512], onesr128[64:65, 0:64],
               wvb[oc][64:65, 512:1024], True, True,
               (ONES, B["wvb"][oc]), sub, tp=(64, 64))
            p.op("act", lambda e, grp=grp: e.copy(out=bcgr_s[:], in_=grp[:, 0:512]),
                 reads=(grb,), writes=(B["bcgr_s"],))
            p.op("dve", lambda e, sup=sup: e.reciprocal(
                out=scr[1][:], in_=sup[:, 0:512]),
                reads=(sub,), writes=(B["scr"][1],))
            p.op("dve", lambda e, oc=oc, glp=glp: e.tensor_mul(
                out=scr[0][:], in0=loc_fm[oc][:], in1=glp[:, 0:512]),
                reads=(B["x_fm"][oc], glb), writes=(B["scr"][0],))
            p.op("dve", lambda e: e.tensor_mul(
                out=scrb[:], in0=scr[0][:], in1=scr[1][:]),
                reads=(B["scr"][0], B["scr"][1]), writes=(B["scrb"],))
            p.op("dve", lambda e, rfp_b=rfp_b: e.tensor_mul(
                out=scr[0][:], in0=rfp_b[:, 0:512], in1=bcgr_s[:]),
                reads=(rfb, B["bcgr_s"]), writes=(B["scr"][0],))
            p.op("dve", lambda e, oc=oc: e.tensor_add(
                out=gated[oc][:], in0=scrb[:], in1=scr[0][:]),
                reads=(B["scrb"], B["scr"][0]), writes=(B["q_fm"][oc],))
        tap("gated", gated, 512, t, bufs=B["q_fm"])
        tap("glsc", [glsc], 512, t, bufs=[B["glsc"]], rows=16)
        tap("grsc", [grsc], 512, t, bufs=[B["grsc"]], rows=16)

        for dc in range(DC):
            p.dma("sp", lambda e, dc=dc: e.dma_start(out=wob[dc][:], in_=s_wout[dc]),
                  reads=(WR,), writes=(B["wvb"][dc],))

        for tc in range(TC):
            pk = [ps[0 + 2 * (tc % 2)], ps[1 + 2 * (tc % 2)]]
            bk = [B["ps"][0 + 2 * (tc % 2)], B["ps"][1 + 2 * (tc % 2)]]
            for dc in range(DC):
                for half in range(2):
                    mm(None, pk[half][:, 0:512],
                       gated[dc][:, tc * 128:(tc + 1) * 128],
                       wob[dc][:, half * 512:(half + 1) * 512],
                       dc == 0, dc == DC - 1,
                       (B["q_fm"][dc], B["wvb"][dc]), bk[half])
            ri_ = tc % 2
            for half in range(2):
                p.op("dve", lambda e, tc=tc, half=half, pp=pk[half], ri_=ri_: e.tensor_add(
                    out=rfp[ri_][:, half * 512:(half + 1) * 512],
                    in0=pp[:, 0:512],
                    in1=x_t[tc][:, half * 512:(half + 1) * 512]),
                    reads=(bk[half], B["x_t"][tc]), writes=(B["rfp"][ri_],))
            for hf in range(2):
                p.op("dve", lambda e, ri_=ri_, hf=hf: e.bn_stats(
                    out=st6[:, hf * 6:(hf + 1) * 6],
                    in_=rfp[ri_][:, hf * 512:(hf + 1) * 512]),
                    reads=(B["rfp"][ri_],), writes=(B["st6"],), fence=True)
            p.op("dve", lambda e: e.bn_aggr(out=st2[:], in_=st6[:]),
                 reads=(B["st6"],), writes=(B["st2"],), fence=True)
            p.op("dve", lambda e: e.tensor_scalar(
                out=rowa[:], in0=st2[:, 1:2], scalar1=1e-5, scalar2=None, op0=ALU.add),
                reads=(B["st2"],), writes=(B["rowa"],))
            p.op("act", lambda e: e.activation(out=rowa[:], in_=rowa[:], func=AF.Sqrt),
                 writes=(B["rowa"],))
            p.op("dve", lambda e: e.reciprocal(out=rowb[:], in_=rowa[:]),
                 reads=(B["rowa"],), writes=(B["rowb"],))
            p.op("dve", lambda e, tc=tc, ri_=ri_: e.tensor_scalar(
                out=y1[tc][:], in0=rfp[ri_][:], scalar1=st2[:, 0:1],
                scalar2=rowb[:], op0=ALU.subtract, op1=ALU.mult),
                reads=(B["rfp"][ri_], B["st2"], B["rowb"]), writes=(B["y1"][tc],))
        tap("y1", y1, 1024, t, bufs=B["y1"])

        if t + 1 < n_tiles:
            load_x(t + 1)

        for dc in range(DC):
            ti = 4 + dc % 2
            tb = B["ps"][ti]
            for tc in range(TC):
                p.op("pe", lambda e, dc=dc, tc=tc, b=ps[ti]: e.transpose(
                    b[:, tc * 128:(tc + 1) * 128],
                    y1[tc][:, dc * 128:(dc + 1) * 128], ident_f32[:]),
                    reads=(B["y1"][tc], CONST), writes=(tb,))
            p.op("dve", lambda e, dc=dc, b=ps[ti]: e.tensor_copy(
                out=y18[dc][:], in_=b[:]),
                reads=(tb,), writes=(B["y18"][dc],))
        tap("y18", y18, 512, t, bufs=B["y18"])

        for hh in range(HC):
            wb = B["w1s"][hh % 2]
            p.dma("sp", lambda e, hh=hh: e.dma_start(out=w1s[hh % 2][:], in_=s_w1[hh]),
                  reads=(WR,), writes=(wb,))
            hi = 5 + hh % 2
            hb = B["ps"][hi]
            for dc in range(DC):
                mm(None, ps[hi][:, 0:512],
                   w1s[hh % 2][:, dc * 128:(dc + 1) * 128], y18[dc][:],
                   dc == 0, dc == DC - 1, (wb, B["y18"][dc]), hb)
            p.op("dve", lambda e, hh=hh, b=ps[hi]: e.tensor_scalar(
                out=h16s[hh % 2][:], in0=b[:, 0:512], scalar1=b1q[:, hh:hh + 1],
                scalar2=None, op0=ALU.add),
                reads=(hb, B["b1q"]), writes=(B["h16s"][hh % 2],))
            p.op("dve", lambda e, hh=hh: e.tensor_scalar(
                out=h8[hh][:], in0=h16s[hh % 2][:], scalar1=0.0, scalar2=None,
                op0=ALU.max),
                reads=(B["h16s"][hh % 2],), writes=(B["h8"][hh],))

        tap("h8", h8, 512, t, bufs=B["h8"])
        if "sw1" in taps and t == 0:
            dd1 = nc.dram_tensor("tap_sw1", [2, 128, 1024], F8, kind="ExternalOutput")
            for j, hh_ in enumerate([0, 25]):
                p.dma("sp", lambda e, j=j, hh_=hh_: e.dma_start(
                    out=dd1[j], in_=s_w1[hh_]), reads=(WR,))
        for half2 in range(2):
            tcs = [2 * half2, 2 * half2 + 1]
            pk = {tc: [ps[0 + 2 * (tc % 2)], ps[1 + 2 * (tc % 2)]] for tc in tcs}
            bk = {tc: [B["ps"][0 + 2 * (tc % 2)], B["ps"][1 + 2 * (tc % 2)]] for tc in tcs}
            for tc in tcs:
                for hf in range(2):
                    mm(None, pk[tc][hf][:, 0:512], ones8[0:1, 0:128],
                       b2q8[0:1, hf * 512:(hf + 1) * 512],
                       True, False, (B["b2q8"],), bk[tc][hf])
            for hh in range(HC):
                wb = B["w2s"][hh % 2]
                p.dma("sp", lambda e, hh=hh: e.dma_start(
                    out=w2s[hh % 2][:], in_=s_w2[hh]), reads=(WR,), writes=(wb,))
                for tc in tcs:
                    for hf in range(2):
                        mm(None, pk[tc][hf][:, 0:512],
                           h8[hh][:, tc * 128:(tc + 1) * 128],
                           w2s[hh % 2][:, hf * 512:(hf + 1) * 512],
                           False, hh == HC - 1, (B["h8"][hh], wb), bk[tc][hf])
            for tc in tcs:
                mi = tc % 2
                p.dma("sp", lambda e, tc=tc, mi=mi, g0=g0: e.dma_start(
                    out=mb[mi][:], in_=dmask[g0 + tc * 128: g0 + (tc + 1) * 128, :]),
                    writes=(B["mb"][mi],))
                for hf in range(2):
                    p.op("dve", lambda e, tc=tc, hf=hf, pp=pk[tc][hf], mi=mi: e.tensor_copy(
                        out=y16s[mi][:, hf * 512:(hf + 1) * 512], in_=pp[:, 0:512]),
                        reads=(bk[tc][hf],), writes=(B["y16s"][mi],))
                ri_ = tc % 2
                p.op("dve", lambda e, mi=mi, ri_=ri_: e.tensor_mul(
                    out=rfp[ri_][:], in0=y16s[mi][:], in1=mb[mi][:]),
                    reads=(B["y16s"][mi], B["mb"][mi]), writes=(B["rfp"][ri_],))
                p.op("dve", lambda e, tc=tc, ri_=ri_: e.tensor_add(
                    out=rfp[ri_][:], in0=rfp[ri_][:], in1=y1[tc][:]),
                    reads=(B["y1"][tc],), writes=(B["rfp"][ri_],))
                for hf in range(2):
                    p.op("dve", lambda e, ri_=ri_, hf=hf: e.bn_stats(
                        out=st6[:, hf * 6:(hf + 1) * 6],
                        in_=rfp[ri_][:, hf * 512:(hf + 1) * 512]),
                        reads=(B["rfp"][ri_],), writes=(B["st6"],), fence=True)
                p.op("dve", lambda e: e.bn_aggr(out=st2[:], in_=st6[:]),
                     reads=(B["st6"],), writes=(B["st2"],), fence=True)
                p.op("dve", lambda e: e.tensor_scalar(
                    out=rowa[:], in0=st2[:, 1:2], scalar1=1e-5, scalar2=None,
                    op0=ALU.add), reads=(B["st2"],), writes=(B["rowa"],))
                p.op("act", lambda e: e.activation(out=rowa[:], in_=rowa[:],
                                                   func=AF.Sqrt), writes=(B["rowa"],))
                p.op("dve", lambda e: e.reciprocal(out=rowb[:], in_=rowa[:]),
                     reads=(B["rowa"],), writes=(B["rowb"],))
                ob = 0
                p.op("dve", lambda e, ri_=ri_, ob=ob: e.tensor_scalar(
                    out=outb[ob][:], in0=rfp[ri_][:], scalar1=st2[:, 0:1],
                    scalar2=rowb[:], op0=ALU.subtract, op1=ALU.mult),
                    reads=(B["rfp"][ri_], B["st2"], B["rowb"]),
                    writes=(B["outb"][ob],))
                p.dma("sp", lambda e, tc=tc, ob=ob, g0=g0: e.dma_start(
                    out=dout[g0 + tc * 128: g0 + (tc + 1) * 128, :], in_=outb[ob][:]),
                    reads=(B["outb"][ob],))

        if tt != TPS - 1 and t + 1 < n_tiles:
            for ch in range(DC):
                p.op("gp", lambda e, ch=ch: e.tensor_copy(
                    out=k_buf[ch][:, 0:256], in_=k_buf[ch][:, 512:768]),
                    writes=(B["k_buf"][ch],))
            for i in range(2):
                p.op("gp", lambda e, i=i: e.tensor_copy(
                    out=v_buf[i][:], in_=v_buf[i + 4][:]),
                    reads=(B["v_buf"][i + 4],), writes=(B["v_buf"][i],))

    total_sp = p.dma_n["sp"]
    p.items["sp"].append(["w", "d:sp", 16 * total_sp])
    total_gp = p.dma_n["gp"]
    if total_gp:
        p.items["gp"].append(["w", "d:gp", 16 * total_gp])
    total_aq = p.dma_n["act"]
    if total_aq:
        p.items["act"].append(["w", "d:act", 16 * total_aq])

    with (
        nc.semaphore("sem_pe") as sem_pe,
        nc.semaphore("sem_act") as sem_act,
        nc.semaphore("sem_dve") as sem_dve,
        nc.semaphore("sem_gp") as sem_gp,
        nc.semaphore("sem_sp") as sem_sp,
        nc.semaphore("dma_sp") as dma_sp,
        nc.semaphore("dma_gp") as dma_gp,
        nc.semaphore("dma_act") as dma_act,
        nc.Block() as block,
    ):
        sems = {"pe": sem_pe, "act": sem_act, "dve": sem_dve, "gp": sem_gp,
                "sp": sem_sp}
        dma_sems = {"sp": dma_sp, "gp": dma_gp, "act": dma_act}
        allnums = sorted(s.num for s in [sem_pe, sem_act, sem_dve, sem_gp,
                                         sem_sp, dma_sp, dma_gp, dma_act])
        clr = range(allnums[0], allnums[-1] + 1)

        def delay(eng, n=1200):
            for _i in range(n):
                eng.nop()

        @block.tensor
        def _(eng):
            delay(eng)
            p.replay("pe", eng, sems, dma_sems)

        @block.scalar
        def _(eng):
            delay(eng)
            p.replay("act", eng, sems, dma_sems)

        @block.vector
        def _(eng):
            delay(eng)
            p.replay("dve", eng, sems, dma_sems)

        @block.gpsimd
        def _(eng):
            eng.sem_clear(clr)
            delay(eng, 200)
            p.replay("gp", eng, sems, dma_sems)

        @block.sync
        def _(eng):
            delay(eng)
            p.replay("sp", eng, sems, dma_sems)

    return nc


_MASK_CACHE = [None]


def _dropout_mask():
    if _MASK_CACHE[0] is None:
        import jax
        with jax.default_device(jax.local_devices(backend="cpu")[0]):
            keep = np.asarray(jax.random.bernoulli(
                jax.random.key(42, impl="threefry2x32"), 0.8, (16, 4096, 1024)))
        _MASK_CACHE[0] = np.where(keep, np.float16(1.25), np.float16(0.0))
    return _MASK_CACHE[0]


def kernel(L0_attn_x, L0_attn_wq, L0_attn_wk, L0_attn_wv, L0_attn_wgq,
           L0_attn_wgk, L0_attn_wgv, L0_attn_r_out_w, L0_attn_gate_local_w,
           L0_attn_gate_remote_w, L0_attn_w_out, L0_ffn_w1, L0_ffn_b1,
           L0_ffn_w2, L0_ffn_b2):
    x = np.ascontiguousarray(L0_attn_x, dtype=np.float32)
    mask = _dropout_mask()

    glbd = np.zeros((D, 16), np.float32)
    grbd = np.zeros((D, 16), np.float32)
    for h in range(16):
        glbd[h * 64:(h + 1) * 64, h] = np.asarray(L0_attn_gate_local_w)[h]
        grbd[h * 64:(h + 1) * 64, h] = np.asarray(L0_attn_gate_remote_w)[h]

    w1t = np.ascontiguousarray(np.asarray(L0_ffn_w1, np.float32).T)
    w2t = np.ascontiguousarray(np.asarray(L0_ffn_w2, np.float32).T)

    shared = dict(
        wq=np.ascontiguousarray(L0_attn_wq, np.float32),
        wk=np.ascontiguousarray(L0_attn_wk, np.float32),
        wv=np.ascontiguousarray(L0_attn_wv, np.float32),
        wgq=np.ascontiguousarray(L0_attn_wgq, np.float32),
        wgk=np.ascontiguousarray(L0_attn_wgk, np.float32),
        wgv=np.ascontiguousarray(L0_attn_wgv, np.float32),
        r_out_w=np.ascontiguousarray(L0_attn_r_out_w, np.float32),
        glbd=glbd, grbd=grbd,
        w_out=np.ascontiguousarray(L0_attn_w_out, np.float32),
        w1t=w1t, w2t=w2t,
        b1=np.ascontiguousarray(L0_ffn_b1, np.float32),
        b2=np.ascontiguousarray(L0_ffn_b2, np.float32),
    )

    nc = build_program(taps=tuple(TAPS), n_tiles=N_TILES_BUILD)
    in_maps = []
    for c in range(NCORE):
        m = dict(shared)
        m["x"] = np.ascontiguousarray(x[c * BPC:(c + 1) * BPC].reshape(TOK, D))
        m["dmask"] = np.ascontiguousarray(mask[c * BPC:(c + 1) * BPC].reshape(TOK, D))
        in_maps.append(m)
    res = run_bass_kernel_spmd(nc, in_maps, core_ids=list(range(NCORE)))
    out = np.empty((16, 4096, 1024), np.float32)
    for c in range(NCORE):
        out[c * BPC:(c + 1) * BPC] = np.asarray(
            res.results[c]["out"], np.float32).reshape(BPC, S, D)
    _LAST_RESULTS[0] = res
    return out


_LAST_RESULTS = [None]
